# version 08
# baseline (speedup 1.0000x reference)
import numpy as np
import ml_dtypes
import concourse.bass as bass
import concourse.mybir as mybir
from concourse.bass_utils import run_bass_kernel_spmd

F32 = mybir.dt.float32
BF16 = mybir.dt.bfloat16
AF = mybir.ActivationFunctionType
ALU = mybir.AluOpType
AX = mybir.AxisListType

D = 2048
T = 2048
NB = D // 128
NTC = T // 512
NH = 16
DH = 128
NE = 8
DFF = 5632
NFB = DFF // 128
R = 16

ENGINES = ("sync", "gpsimd", "tensor", "vector", "scalar")
ENG_SEM = {"tensor": "spe", "vector": "sdv", "scalar": "sac"}
DMA_SEMS = ("dsy", "dgp")


class Prog:

    def __init__(self, nc):
        self.nc = nc
        self.ops = {e: [] for e in ENGINES}
        self.counts = {}
        self.lastwait = {e: {} for e in ENGINES}
        self.handles = {}

    def _wait_filter(self, eng, waits, strict=False):
        ws = []
        for tok in waits:
            if tok is None:
                continue
            (s, v) = tok
            if ENG_SEM.get(eng) == s and not strict:
                continue
            if self.lastwait[eng].get(s, 0) >= v:
                continue
            self.lastwait[eng][s] = v
            ws.append((s, v))
        return ws

    def emit(self, eng, fn, waits=(), inc=None, inc_amt=1, strict=False):
        ws = self._wait_filter(eng, waits, strict)
        tok = None
        if inc is not None:
            self.counts[inc] = self.counts.get(inc, 0) + inc_amt
            tok = (inc, self.counts[inc])
        self.ops[eng].append([ws, fn, inc, inc_amt])
        return tok

    def dma(self, queue, dst, src, waits=(), **kw):
        sem = "dsy" if queue == "sync" else "dgp"
        return self.emit(queue, lambda e: e.dma_start(dst, src, **kw),
                         waits=waits, inc=sem, inc_amt=16)

    def pe(self, fn, waits=(), inc=False, strict=False):
        return self.emit("tensor", fn, waits, "spe" if inc else None,
                         strict=strict)

    def dv(self, fn, waits=(), inc=False, strict=False):
        return self.emit("vector", fn, waits, "sdv" if inc else None,
                         strict=strict)

    def ac(self, fn, waits=(), inc=False, strict=False):
        return self.emit("scalar", fn, waits, "sac" if inc else None,
                         strict=strict)

    def barrier(self):
        toks = []
        for eng, sem in ENG_SEM.items():
            lst = self.ops[eng]
            last = None
            for op in reversed(lst):
                if op[1] is not None:
                    last = op
                    break
            if last is not None:
                if last[2] is None:
                    last[2] = sem
                    last[3] = 1
                    self.counts[sem] = self.counts.get(sem, 0) + 1
                toks.append((sem, self.counts.get(sem, 0)))
        for s in DMA_SEMS:
            if self.counts.get(s, 0):
                toks.append((s, self.counts[s]))
        for eng in ENGINES:
            if self.ops[eng] or eng in ("sync", "gpsimd", "tensor", "vector", "scalar"):
                ws = self._wait_filter(eng, toks)
                if ws:
                    self.ops[eng].append([ws, None, None, 1])
        return toks

    def finalize(self):
        nc = self.nc
        import contextlib
        with contextlib.ExitStack() as es:
            for s in list(ENG_SEM.values()) + list(DMA_SEMS):
                self.handles[s] = es.enter_context(nc.semaphore(s))
            blk = es.enter_context(nc.Block())
            for eng in ENGINES:
                if not self.ops[eng]:
                    continue

                def body(e, eng=eng):
                    for ws, fn, inc, amt in self.ops[eng]:
                        for (s, v) in ws:
                            e.wait_ge(self.handles[s], v)
                        if fn is not None:
                            inst = fn(e)
                            if inc is not None:
                                inst.then_inc(self.handles[inc], amt)
                getattr(blk, eng)(body)


def col128(v):
    v = np.asarray(v)
    return np.ascontiguousarray(v.reshape(-1, 128).T)


def prep_host(inp):
    h = {}
    x = np.asarray(inp["L0_attn_x"])
    h["_x_per_core"] = [np.ascontiguousarray(x[b].T) for b in range(8)]
    h["wqkv"] = np.asarray(inp["L0_attn_W_qkv"])
    h["bqkv"] = col128(inp["L0_attn_b_qkv"])
    h["wout"] = np.asarray(inp["L0_attn_W_out"])
    h["bout"] = col128(inp["L0_attn_b_out"])
    h["gatew"] = np.asarray(inp["L0_ffn_gate_w"])
    h["fc1"] = np.asarray(inp["L0_ffn_fc1_w"])
    h["fc1b"] = np.stack([col128(b) for b in np.asarray(inp["L0_ffn_fc1_b"])])
    h["fc2"] = np.asarray(inp["L0_ffn_fc2_w"])
    h["fc2b"] = np.asarray(inp["L0_ffn_fc2_b"])
    h["alpha"] = np.asarray(inp["L0_ffn_alpha"]).reshape(1, 8)
    for k in ("ln1_w", "ln1_b", "ln2_w", "ln2_b", "tm_u", "tm_w", "tm_mix_k",
              "tm_mix_v", "tm_mix_r", "tm_bo", "cm_mix_k", "cm_mix_r"):
        h[k.replace("_", "")] = col128(inp[f"L1_attn_{k}"])
    for k in ("tm_Wk", "tm_Wv", "tm_Wr", "tm_Wo", "cm_Wk", "cm_Wv", "cm_Wr"):
        h[k.replace("_", "").lower()] = np.ascontiguousarray(np.asarray(inp[f"L1_attn_{k}"]).T)
    h["wgT"] = np.ascontiguousarray(np.asarray(inp["L1_ffn_Wg"]).T)
    h["wuT"] = np.ascontiguousarray(np.asarray(inp["L1_ffn_Wu"]).T)
    h["wdT"] = np.ascontiguousarray(np.asarray(inp["L1_ffn_Wd"]).T)
    h["wr8T"] = np.ascontiguousarray(np.asarray(inp["L1_ffn_Wr"]).T)
    h["agT"] = np.ascontiguousarray(np.concatenate(
        [np.asarray(inp["L1_ffn_Ag"])[e].T for e in range(NE)], axis=1))
    h["auT"] = np.ascontiguousarray(np.concatenate(
        [np.asarray(inp["L1_ffn_Au"])[e].T for e in range(NE)], axis=1))
    h["bgT"] = np.ascontiguousarray(np.stack(
        [np.asarray(inp["L1_ffn_Bg"])[e].T for e in range(NE)], axis=1))
    h["buT"] = np.ascontiguousarray(np.stack(
        [np.asarray(inp["L1_ffn_Bu"])[e].T for e in range(NE)], axis=1))
    h["adT"] = np.ascontiguousarray(np.stack(
        [np.asarray(inp["L1_ffn_Ad"])[e].T for e in range(NE)], axis=1))
    h["bdT"] = np.ascontiguousarray(np.concatenate(
        [np.asarray(inp["L1_ffn_Bd"])[e].T for e in range(NE)], axis=0))
    bgts = np.zeros((128, 3 * DFF), dtype=np.float32)
    buts = np.zeros((128, 3 * DFF), dtype=np.float32)
    Bg = np.asarray(inp["L1_ffn_Bg"]); Bu = np.asarray(inp["L1_ffn_Bu"])
    for e in range(NE):
        rr = (e % 3) * 32
        cc = (e // 3) * DFF
        bgts[rr:rr + 16, cc:cc + DFF] = Bg[e].T
        buts[rr:rr + 16, cc:cc + DFF] = Bu[e].T
    h["bgts"] = bgts
    h["buts"] = buts
    h["sel16r"] = np.repeat(np.eye(8, dtype=np.float32), 16, axis=1)
    h["tri"] = np.tril(np.ones((128, 128), dtype=np.float32)).T.copy()
    h["identity"] = np.eye(128, dtype=np.float32)
    return h


SCRATCH = {
    "xn0b": BF16, "xn0f": F32, "out": F32,
    "s0sc": F32, "g0sc": F32, "x1sc": F32, "xn1f": F32, "xn1b": BF16,
    "xinsc": F32, "s1sc": F32, "base1": F32,
}

IN_SPECS = {
    "xT": ([D, T], F32),
    "wqkv": ([D, 3 * D], F32), "bqkv": ([128, 48], F32),
    "wout": ([D, D], F32), "bout": ([128, 16], F32),
    "gatew": ([D, 8], F32),
    "fc1": ([NE, D, D], F32), "fc1b": ([NE, 128, 16], F32),
    "fc2": ([NE, D, D], F32), "fc2b": ([NE, D], F32),
    "alpha": ([1, 8], F32),
    "ln1w": ([128, 16], F32), "ln1b": ([128, 16], F32),
    "ln2w": ([128, 16], F32), "ln2b": ([128, 16], F32),
    "tmu": ([128, 16], F32), "tmw": ([128, 16], F32),
    "tmmixk": ([128, 16], F32), "tmmixv": ([128, 16], F32),
    "tmmixr": ([128, 16], F32), "tmbo": ([128, 16], F32),
    "cmmixk": ([128, 16], F32), "cmmixr": ([128, 16], F32),
    "tmwk": ([D, D], F32), "tmwv": ([D, D], F32), "tmwr": ([D, D], F32),
    "tmwo": ([D, D], F32),
    "cmwk": ([D, D], F32), "cmwv": ([D, D], F32), "cmwr": ([D, D], F32),
    "wgT": ([D, DFF], F32), "wuT": ([D, DFF], F32), "wdT": ([DFF, D], F32),
    "wr8T": ([D, 8], F32),
    "agT": ([D, 128], F32), "auT": ([D, 128], F32),
    "bgT": ([R, NE, DFF], F32), "buT": ([R, NE, DFF], F32),
    "adT": ([DFF, NE, R], F32), "bdT": ([128, D], F32),
    "bgts": ([128, 3 * DFF], F32), "buts": ([128, 3 * DFF], F32),
    "sel16r": ([8, 128], F32),
    "tri": ([128, 128], F32), "identity": ([128, 128], F32),
}

PHASE_NEEDS = {
    "ln0": ["xT"],
    "attn0": ["wqkv", "bqkv", "wout", "bout", "tri"],
    "moe0": ["gatew", "fc1", "fc1b", "fc2", "fc2b", "alpha", "identity", "sel"],
    "epi0": ["xT"],
    "rwkv1": ["ln1w", "ln1b", "ln2w", "ln2b", "tmu", "tmw", "tmmixk", "tmmixv",
              "tmmixr", "tmbo", "cmmixk", "cmmixr", "tmwk", "tmwv", "tmwr",
              "tmwo", "cmwk", "cmwv", "cmwr"],
    "ffn1": ["wgT", "wuT", "wdT", "wr8T", "agT", "auT", "bgts", "buts", "adT",
             "bdT", "identity", "sel16r"],
    "epi1": [],
}


class Dev:

    def __init__(self, nc, p, needs, sin, sout):
        self.nc, self.p = nc, p
        self.tn = {}
        for name in needs:
            shape, dt = IN_SPECS[name]
            self.tn[name] = nc.dram_tensor(name, shape, dt, kind="ExternalInput").ap()
        for name in sin:
            self.tn[name] = nc.dram_tensor(name, [D, T], SCRATCH[name],
                                           kind="ExternalInput").ap()
        for name in sout:
            kind = "ExternalOutput" if name == "out" else sout[name]
            dt = F32 if name == "out" else SCRATCH[name]
            self.tn[name] = nc.dram_tensor(name, [D, T], dt, kind=kind).ap()
        self.R = nc.alloc_sbuf_tensor("arR", [128, NB, T], BF16).ap()
        self.W = nc.alloc_sbuf_tensor("arW", [128, NB, T], BF16).ap()
        self.F = nc.alloc_sbuf_tensor("arF", [128, 7, T], F32).ap()
        self.PS = nc.alloc_psum_tensor("psA", [128, 4096], F32).ap()
        self.ones1x128 = nc.alloc_sbuf_tensor("ones1x128", [1, 128], F32).ap()
        self.onesb = nc.alloc_sbuf_tensor("onesb", [128, 1], BF16).ap()
        self.onesf = nc.alloc_sbuf_tensor("onesf", [128, 1], F32).ap()
        self.ccol = {}
        self._ccol_t = nc.alloc_sbuf_tensor("ccols", [128, 16], F32).ap()
        self._ccol_n = 0
        self.smalls = {}

    def fslot(self, i):
        return self.F[:, i, :]

    def fslot_bf(self, i):
        return self.F[:, i, :].bitcast(BF16)

    def bank(self, b, n=512):
        return self.PS[:, b * 512: b * 512 + n]

    def ccol_init(self, vals):
        p = self.p
        toks = []
        for v in vals:
            if v in self.ccol:
                continue
            i = self._ccol_n
            self._ccol_n += 1
            ap = self._ccol_t[:, i:i + 1]
            self.ccol[v] = ap
            toks.append(p.dv(lambda e, ap=ap, v=v: e.memset(ap, v), inc=True))
        t1 = p.dv(lambda e: e.memset(self.ones1x128, 1.0), inc=True)
        t2 = p.dv(lambda e: e.memset(self.onesb, 1.0), inc=True)
        t3 = p.dv(lambda e: e.memset(self.onesf, 1.0), inc=True)
        self.const_tok = t3
        return t3


def ln_phase(C, src, dst_sbuf=None, affine=None, out_f32=None, out_bf16=None,
             fp32_stats=False, skip_stats=False, stats_loc=None):
    p, PS = C.p, C.PS
    xb = C.fslot_bf(0)
    xb2 = C.fslot_bf(1)
    cons = [[], []]
    bufcons = [[], []]
    petok = None
    for db in ([] if skip_stats else range(NB)):
        sl = db % 2
        fx = C.fslot(2 + sl)
        ld = p.dma("sync", fx, src[db * 128:(db + 1) * 128, :], waits=cons[sl])
        if fp32_stats:
            fsq = C.fslot(4 + sl)
            tsq = p.ac(lambda e, f=fx, g=fsq: e.activation(g, f, AF.Square),
                       waits=[ld] + bufcons[sl], inc=True)
            cons[sl] = [tsq]
            for tc in range(4):
                p.pe(lambda e, f=fx, tc=tc, db=db: e.matmul(
                    PS[0:1, tc * 512:(tc + 1) * 512], C.onesf,
                    f[:, tc * 512:(tc + 1) * 512],
                    start=(db == 0), stop=(db == NB - 1)),
                    waits=[C.const_tok, ld, tsq] + bufcons[sl])
            for tc in range(4):
                petok = p.pe(lambda e, g=fsq, tc=tc, db=db: e.matmul(
                    PS[0:1, 2048 + tc * 512: 2048 + (tc + 1) * 512], C.onesf,
                    g[:, tc * 512:(tc + 1) * 512],
                    start=(db == 0), stop=(db == NB - 1)),
                    waits=[tsq], inc=(tc == 3))
            bufcons[sl] = [petok]
            cons[sl] = [petok]
            continue
        buf = (xb, xb2)[sl]
        tcast = p.dv(lambda e, b=buf, f=fx: e.tensor_copy(b[:, 0:2048], f),
                     waits=[ld] + bufcons[sl], inc=True)
        tsq = p.ac(lambda e, b=buf, f=fx: e.activation(b[:, 2048:4096], f, AF.Square),
                   waits=[ld] + bufcons[sl], inc=True)
        cons[sl] = [tcast, tsq]
        for tc in range(4):
            p.pe(lambda e, b=buf, tc=tc, db=db: e.matmul(
                PS[0:1, tc * 512:(tc + 1) * 512], C.onesb,
                b[:, tc * 512:(tc + 1) * 512],
                start=(db == 0), stop=(db == NB - 1)), waits=[tcast, tsq])
        for tc in range(4):
            petok = p.pe(lambda e, b=buf, tc=tc, db=db: e.matmul(
                PS[0:1, 2048 + tc * 512: 2048 + (tc + 1) * 512], C.onesb,
                b[:, 2048 + tc * 512: 2048 + (tc + 1) * 512],
                start=(db == 0), stop=(db == NB - 1)),
                waits=[tcast, tsq], inc=(tc == 3))
        bufcons[sl] = [petok]
    lnf = C.F.rearrange("p a b -> p (a b)")
    pS1 = stats_loc[0] if stats_loc else PS[0:1, 0:2048]
    pS2 = stats_loc[1] if stats_loc else PS[0:1, 2048:4096]
    rS1, rS2 = lnf[0:1, 0:2048], lnf[0:1, 2048:4096]
    rMean = lnf[0:1, 12288:14336]
    rVar = lnf[0:1, 0:2048]
    rRstd = lnf[0:1, 2048:4096]
    rMr = lnf[0:1, 0:2048]
    p.dv(lambda e: e.tensor_copy(rS1, pS1), waits=[petok])
    p.dv(lambda e: e.tensor_copy(rS2, pS2))
    p.dv(lambda e: e.tensor_scalar_mul(rMean, rS1, 1.0 / D))
    p.dv(lambda e: e.tensor_mul(rVar, rMean, rMean))
    tv = p.dv(lambda e: e.scalar_tensor_tensor(rVar, rS2, 1.0 / D, rVar,
                                               ALU.mult, ALU.subtract), inc=True)
    eps = C.ccol[1e-5][0:1, :]
    p.ac(lambda e: e.activation(rRstd, rVar, AF.Ln, bias=eps), waits=[tv])
    ta = p.ac(lambda e: e.activation(rRstd, rRstd, AF.Exp, scale=-0.5), inc=True)
    trows = p.dv(lambda e: e.tensor_mul(rMr, rMean, rRstd), waits=[ta], inc=True)
    bt = None
    for tc in range(4):
        p.pe(lambda e, tc=tc: e.matmul(
            PS[:, tc * 512:(tc + 1) * 512], C.ones1x128,
            rRstd[:, tc * 512:(tc + 1) * 512], start=True, stop=True),
            waits=[trows])
        bt = p.pe(lambda e, tc=tc: e.matmul(
            PS[:, 2048 + tc * 512: 2048 + (tc + 1) * 512], C.ones1x128,
            rMr[:, tc * 512:(tc + 1) * 512], start=True, stop=True),
            waits=[trows], inc=(tc == 3))
    consx = [cons[0], cons[1]]
    const_ = [[], []]
    for db in range(NB):
        sl = db % 2
        fx, ftmp = C.fslot(2 + sl), C.fslot(4 + sl)
        ld = p.dma("sync", fx, src[db * 128:(db + 1) * 128, :], waits=consx[sl])
        t1 = p.dv(lambda e, f=fx, g=ftmp: e.tensor_mul(g, f, PS[:, 0:2048]),
                  waits=[ld, bt] + const_[sl], inc=True)
        consx[sl] = [t1]
        t2 = p.dv(lambda e, g=ftmp: e.tensor_sub(g, g, PS[:, 2048:4096]), inc=True)
        tcons = [t2]
        if out_f32 is not None:
            tcons.append(p.dma("sync", out_f32[db * 128:(db + 1) * 128, :], ftmp,
                               waits=[t2]))
        dst = dst_sbuf(db) if dst_sbuf else None
        if affine:
            wv, bv = affine
            t3 = p.dv(lambda e, g=ftmp, db=db, dst=dst: e.tensor_scalar(
                dst, g, wv[:, db:db + 1], bv[:, db:db + 1], ALU.mult, ALU.add),
                inc=True)
            tcons.append(t3)
        elif dst is not None:
            t3 = p.dv(lambda e, g=ftmp, dst=dst: e.tensor_copy(dst, g), inc=True)
            tcons.append(t3)
        else:
            t3 = t2
        if out_bf16 is not None and dst is not None:
            tcons.append(p.dma("sync", out_bf16[db * 128:(db + 1) * 128, :], dst,
                               waits=[t3]))
        const_[sl] = tcons


SCALE_QK = 1.0 / float(np.sqrt(DH))


def attn0_phase(C, xn_blocks, s_out_hbm):
    import os
    STAGE = int(os.environ.get("ATTN_STAGE", "5"))
    p, PS = C.p, C.PS
    Ff = C.F.rearrange("p a b -> p (a b)")
    FB = Ff.bitcast(BF16)
    SW = [FB[:, 0:6144].rearrange("p (a b) -> p a b", b=384),
          FB[:, 6144:12288].rearrange("p (a b) -> p a b", b=384)]
    QH = FB[:, 12288:14336]
    KH = FB[:, 14336:16384]
    VH = FB[:, 16384:18432].rearrange("p (a b) -> p a b", b=128)
    PT = [FB[:, 18432:18944], FB[:, 18944:19456]]
    TRI = FB[:, 19456:19584]
    SWO = [FB[:, 20480:22528].rearrange("p (a b) -> p a b", b=128),
           FB[:, 22528:24576].rearrange("p (a b) -> p a b", b=128)]
    SOUT = Ff[:, 12288:14336]
    NRMR = [Ff[0:1, 12288 + i * 512:12288 + (i + 1) * 512] for i in range(4)]
    RR = [Ff[0:1, 0:2048], Ff[0:1, 2048:4096]]
    wqkv = C.tn["wqkv"]
    scale_ap = C.ccol[SCALE_QK]

    ttri = p.dma("gpsimd", TRI, C.tn["tri"])

    sw_cons = [[], []]
    ld_toks = [None, None]

    def load_head(h):
        sl = h % 2
        toks = []
        for j, off in enumerate((h * 128, D + h * 128, 2 * D + h * 128)):
            toks.append(p.dma(
                "gpsimd", SW[sl][:, :, j * 128:(j + 1) * 128],
                wqkv[:, off:off + 128].rearrange("(c p) m -> p c m", p=128),
                waits=sw_cons[sl]))
        ld_toks[sl] = toks

    if STAGE < 1:
        return
    load_head(0)
    nrmd = C.tn["nrmdram"]
    nrm_st = [[], [], [], []]
    evq_tok = [None] * 8
    pv_evac_tok = None
    nrm_evac_tok = None
    sbank_user = [None, None]
    pt_user = [None, None]
    for h in range(NH):
        sl = h % 2
        if h + 1 < NH:
            load_head(h + 1)
        ldt = ld_toks[sl]
        if STAGE < 1.5 and False:
            continue
        qkv_mm_toks = []
        sbq = C.smalls["bqkv"]
        for j, dstbuf in ((0, QH), (1, KH)):
            bcol = sbq[:, 16 * j + h:16 * j + h + 1]
            for tc in range(4):
                bk = tc % 2
                for db in range(NB):
                    tok = p.pe(lambda e, sl=sl, j=j, db=db, tc=tc, bk=bk: e.matmul(
                        C.bank(bk)[:, 0:512], SW[sl][:, db, j * 128:(j + 1) * 128],
                        xn_blocks(db)[:, tc * 512:(tc + 1) * 512],
                        start=(db == 0), stop=(db == NB - 1)),
                        waits=(ldt + [evq_tok[bk]] + C.smalls_tok) if db == 0 else [],
                        inc=(db == NB - 1))
                evq_tok[bk] = p.ac(lambda e, d=dstbuf, tc=tc, bk=bk, bcol=bcol:
                    e.activation(d[:, tc * 512:(tc + 1) * 512],
                                 C.bank(bk)[:, 0:512], AF.Identity, bias=bcol),
                    waits=[tok], inc=True)
                qkv_mm_toks.append(tok)
        for tt in range(NB):
            bk = 2 + tt % 2
            for db in range(NB):
                tok = p.pe(lambda e, sl=sl, db=db, tt=tt, bk=bk: e.matmul(
                    C.bank(bk)[:, 0:128],
                    xn_blocks(db)[:, tt * 128:(tt + 1) * 128],
                    SW[sl][:, db, 256:384],
                    start=(db == 0), stop=(db == NB - 1)),
                    waits=([evq_tok[bk]] if db == 0 else []),
                    inc=(db == NB - 1))
            evq_tok[bk] = p.ac(lambda e, tt=tt, bk=bk: e.copy(
                VH[:, tt, :], C.bank(bk)[:, 0:128]), waits=[tok], inc=True)
            qkv_mm_toks.append(tok)
        sw_cons[sl] = qkv_mm_toks
        qkv_ev = [evq_tok[0], evq_tok[1], evq_tok[2], evq_tok[3]]
        if STAGE < 2:
            continue
        for qc in range(4):
            nkt = 4 * qc + 4
            s_toks = {}
            exp_toks = {}

            def emit_S(kt, qc=qc, s_toks=s_toks):
                par = kt % 2
                w = qkv_ev + [sbank_user[par]]
                s_toks[kt] = p.pe(lambda e, kt=kt, qc=qc: e.matmul(
                    C.bank(4 + kt % 2)[:, 0:512], KH[:, kt * 128:(kt + 1) * 128],
                    QH[:, qc * 512:(qc + 1) * 512], start=True, stop=True),
                    waits=w, inc=True)

            def emit_exp_mask(kt, qc=qc, s_toks=s_toks, exp_toks=exp_toks):
                par = kt % 2
                ptb = PT[par]
                te = p.ac(lambda e, kt=kt, ptb=ptb: e.activation(
                    ptb, C.bank(4 + kt % 2)[:, 0:512], AF.Exp, scale=scale_ap),
                    waits=[s_toks[kt], pt_user[par], C.const_tok], inc=True)
                sbank_user[par] = te
                if kt >= 4 * qc:
                    off = kt * 128 - qc * 512
                    if off > 0:
                        te = p.dv(lambda e, ptb=ptb, off=off: e.memset(
                            ptb[:, 0:off], 0.0), waits=[te], inc=True)
                    te = p.dv(lambda e, ptb=ptb, off=off: e.tensor_mul(
                        ptb[:, off:off + 128], ptb[:, off:off + 128], TRI),
                        waits=[te, ttri], inc=True)
                exp_toks[kt] = te

            def emit_PV(kt, qc=qc, nkt=nkt, exp_toks=exp_toks):
                nonlocal pv_evac_tok, nrm_evac_tok
                par = kt % 2
                ptb = PT[par]
                w = [exp_toks[kt]] + ([pv_evac_tok] if kt == 0 else [])
                p.pe(lambda e, kt=kt, ptb=ptb: e.matmul(
                    C.bank(6)[:, 0:512], VH[:, kt, :], ptb,
                    start=(kt == 0), stop=(kt == nkt - 1)), waits=w)
                w2 = [nrm_evac_tok] if kt == 0 else []
                pt_user[par] = p.pe(lambda e, kt=kt, ptb=ptb, nkt=nkt: e.matmul(
                    PS[0:1, 3584:4096], C.onesb, ptb,
                    start=(kt == 0), stop=(kt == nkt - 1)), waits=w2, inc=True)

            emit_S(0)
            for kt in range(nkt):
                emit_exp_mask(kt)
                if kt + 1 < nkt:
                    emit_S(kt + 1)
                if STAGE >= 3:
                    emit_PV(kt)
            if STAGE < 3:
                continue
            pvt = pt_user[(nkt - 1) % 2]
            pv_evac_tok = p.dv(lambda e, h=h, qc=qc: e.tensor_copy(
                C.W[:, h, qc * 512:(qc + 1) * 512], C.bank(6)[:, 0:512]),
                waits=[pvt], inc=True)
            r = (h * 4 + qc) % 4
            nrm_evac_tok = p.dv(lambda e, r=r: e.tensor_copy(
                NRMR[r], PS[0:1, 3584:4096]),
                waits=[pvt] + nrm_st[r], inc=True)
            nrm_st[r] = [p.dma(
                "sync", nrmd[0:1, (h * 4 + qc) * 512:(h * 4 + qc + 1) * 512],
                NRMR[r], waits=[nrm_evac_tok])]
    p.barrier()
    if STAGE < 4:
        return
    n16a = C.F[:, 0, :][0:16, :]
    n16b = C.F[:, 1, :][0:16, :]
    n16c = C.F[:, 2, :][0:16, :]
    t = p.dma("sync", n16a, nrmd.rearrange("o (a b) -> (o a) b", a=16))
    tr = p.dv(lambda e: e.reciprocal(n16b, n16a), waits=[t], inc=True)
    t = p.dma("sync", nrmd.rearrange("o (a b) -> (o a) b", a=16), n16b,
              waits=[tr])
    p.barrier()
    bc_tok = {}
    rr_cons = [[], []]
    for h in range(NH):
        rsl = h % 2
        lt = p.dma("sync", RR[rsl], nrmd[0:1, h * T:(h + 1) * T],
                   waits=rr_cons[rsl])
        rr_cons[rsl] = []
        for tc in range(4):
            bk = (h * 4 + tc) % 4
            tb = p.pe(lambda e, h=h, tc=tc, bk=bk, rsl=rsl: e.matmul(
                C.bank(bk)[:, 0:512], C.ones1x128,
                RR[rsl][:, tc * 512:(tc + 1) * 512],
                start=True, stop=True), waits=[bc_tok.get(bk), lt], inc=True)
            rr_cons[rsl].append(tb)
            bc_tok[bk] = p.dv(lambda e, h=h, tc=tc, bk=bk: e.tensor_mul(
                C.W[:, h, tc * 512:(tc + 1) * 512],
                C.W[:, h, tc * 512:(tc + 1) * 512], C.bank(bk)[:, 0:512]),
                waits=[tb], inc=True)
    p.barrier()
    if STAGE < 5:
        return
    wout = C.tn["wout"]
    sbout = C.smalls["bout"]
    swo_cons = [[], []]
    ld2 = [None, None]

    def load_wo(ot):
        sl = ot % 2
        ld2[sl] = [p.dma("gpsimd", SWO[sl],
                         wout[:, ot * 128:(ot + 1) * 128].rearrange(
                             "(c p) m -> p c m", p=128), waits=swo_cons[sl])]

    load_wo(0)
    st_tok = []
    ev_tok = {}
    beff_user = [None]
    for ot in range(NB):
        sl = ot % 2
        if ot + 1 < NB:
            load_wo(ot + 1)
        mm_toks = []
        for cb in range(NB):
            btk = p.pe(lambda e, sl=sl, cb=cb: e.matmul(
                C.bank(7)[:, 0:1], SWO[sl][:, cb, :],
                C.smalls["bqkvb"][:, 32 + cb:33 + cb],
                start=(cb == 0), stop=(cb == NB - 1)),
                waits=(ld2[sl] + [beff_user[0]]) if cb == 0 else [],
                inc=(cb == NB - 1))
        beff_user[0] = p.dv(lambda e: e.tensor_copy(
            C.smalls["beff"], C.bank(7)[:, 0:1]), waits=[btk], inc=True)
        beffc = beff_user[0]
        for tc in range(4):
            bk = tc % 2
            for cb in range(NB):
                tok = p.pe(lambda e, sl=sl, cb=cb, tc=tc, bk=bk: e.matmul(
                    C.bank(bk)[:, 0:512], SWO[sl][:, cb, :],
                    C.W[:, cb, tc * 512:(tc + 1) * 512],
                    start=(cb == 0), stop=(cb == NB - 1)),
                    waits=(ld2[sl] + [ev_tok.get(bk)]) if cb == 0 else [],
                    inc=(cb == NB - 1))
            mm_toks.append(tok)
            ev_tok[bk] = p.dv(lambda e, ot=ot, tc=tc, bk=bk: e.tensor_scalar(
                SOUT[:, tc * 512:(tc + 1) * 512], C.bank(bk)[:, 0:512],
                sbout[:, ot:ot + 1], C.smalls["beff"][:, 0:1],
                ALU.add, ALU.add),
                waits=[tok, beffc] + ([st_tok[-1]] if st_tok else []),
                strict=True, inc=True)
        swo_cons[sl] = mm_toks
        st_tok.append(p.dma("sync", s_out_hbm[ot * 128:(ot + 1) * 128, :], SOUT,
                            waits=[ev_tok[0], ev_tok[1]]))
    p.barrier()


def moe0_phase(C, xn_blocks, g_out_hbm):
    import os
    STAGE = int(os.environ.get("MOE_STAGE", "9"))
    p, PS = C.p, C.PS
    Ff = C.F.rearrange("p a b -> p (a b)")
    FB = Ff.bitcast(BF16)
    CB = Ff[:, 0:2048]
    STG = [Ff[:, 2048:4096], Ff[:, 4096:6144]]
    comb = Ff[:, 6656:6784].rearrange("p (a b) -> p a b", b=8)
    gcol = Ff[:, 6784:6800]
    l2s = Ff[:, 6800:6808]
    eqs = Ff[:, 6808:6816]
    W1B = [FB[:, 16384:18432].rearrange("p (a b) -> p a b", b=128),
           FB[:, 18432:20480].rearrange("p (a b) -> p a b", b=128)]
    W2B = [FB[:, 20480:22528].rearrange("p (a b) -> p a b", b=128),
           FB[:, 22528:24576].rearrange("p (a b) -> p a b", b=128)]
    SGW = FB[:, 26624:26752].rearrange("p (a b) -> p a b", b=8)
    comb8b = C.smalls["comb8b"]
    alpt = C.smalls["alphat"]
    sfc2b, sel, idn = C.smalls["fc2b"], C.smalls["sel"], C.smalls["idn"]
    fc1b = C.smalls["fc1b"]
    fc1h, fc2h = C.tn["fc1"], C.tn["fc2"]

    sgwf = C.smalls["gatewf"].rearrange("p (c m) -> p c m", m=8)
    xn0f = C.tn["xn0f"]
    ta = p.pe(lambda e: e.matmul(C.bank(7)[:, 0:8], C.ones1x128,
                                 C.smalls["alpha"], start=True, stop=True),
              waits=C.smalls_tok + [C.const_tok], inc=True)
    ta = p.dv(lambda e: e.tensor_copy(alpt, C.bank(7)[:, 0:8]), waits=[ta], inc=True)
    logacc = Ff[:, 6816:6944]
    gcons = [[], []]
    xfr = [STG[0], STG[1]]
    gacc = None
    for db in range(NB):
        sl = db % 2
        ld = p.dma("sync", xfr[sl], xn0f[db * 128:(db + 1) * 128, :],
                   waits=gcons[sl])
        for tt in range(NB):
            tok = p.pe(lambda e, tt=tt, db=db, sl=sl: e.matmul(
                C.bank(0)[:, tt * 8:(tt + 1) * 8],
                xfr[sl][:, tt * 128:(tt + 1) * 128], sgwf[:, db, :],
                start=True, stop=True),
                waits=[ld, gacc], inc=(tt == NB - 1))
        if db == 0:
            gacc = p.dv(lambda e: e.tensor_copy(logacc, C.bank(0)[:, 0:128]),
                        waits=[tok], inc=True)
        else:
            gacc = p.dv(lambda e: e.tensor_add(logacc, logacc,
                                               C.bank(0)[:, 0:128]),
                        waits=[tok, gacc], strict=True, inc=True)
        gcons[sl] = [tok]
    lg_toks = [gacc] * NB
    if STAGE < 2:
        return
    m1, m2, nm1, den, e2c = (gcol[:, 0:1], gcol[:, 1:2], gcol[:, 2:3],
                             gcol[:, 3:4], gcol[:, 4:5])
    if os.environ.get("MOE_DBG"):
        dbgl = C.nc.dram_tensor("dbgl", [128, 128], F32, kind="ExternalOutput").ap()
        tdl = p.dv(lambda e: e.tensor_copy(Ff[:, 6816:6944], C.bank(0)[:, 0:128]),
                   waits=[lg_toks[NB - 1]], inc=True)
        p.dma("sync", dbgl, Ff[:, 6816:6944], waits=[tdl])
    ct_toks = []
    for tt in range(NB):
        L = logacc[:, tt * 8:(tt + 1) * 8]
        t0 = p.dv(lambda e, L=L: e.reduce_max(m1, L, axis=AX.X),
                  waits=[lg_toks[NB - 1], ta], inc=True)
        t1 = p.dv(lambda e, L=L: e.tensor_single_scalar(eqs, L, m1, ALU.is_ge),
                  waits=[t0], strict=True, inc=True)
        t2 = p.dv(lambda e, L=L: e.scalar_tensor_tensor(l2s, eqs, -1e30, L,
                                                        ALU.mult, ALU.add),
                  waits=[t1], strict=True, inc=True)
        t3 = p.dv(lambda e: e.reduce_max(m2, l2s, axis=AX.X),
                  waits=[t2], strict=True, inc=True)
        t4 = p.dv(lambda e: e.tensor_scalar_mul(nm1, m1, -1.0),
                  waits=[t3], strict=True, inc=True)
        ta2 = p.ac(lambda e, L=L: e.activation(l2s, L, AF.Exp, bias=nm1),
                   waits=[t4])
        ta2 = p.ac(lambda e: e.activation(e2c, m2, AF.Exp, bias=nm1), inc=True)
        t5 = p.dv(lambda e: e.tensor_scalar_add(e2c, e2c, 1.0), waits=[ta2],
                  inc=True)
        t6 = p.dv(lambda e: e.reciprocal(den, e2c), waits=[t5], strict=True,
                  inc=True)
        t7 = p.dv(lambda e, L=L: e.tensor_single_scalar(eqs, L, m2, ALU.is_ge),
                  waits=[t6], strict=True, inc=True)
        t8 = p.dv(lambda e: e.tensor_mul(l2s, l2s, eqs), waits=[t7],
                  strict=True, inc=True)
        t9 = p.dv(lambda e: e.tensor_scalar_mul(l2s, l2s, den), waits=[t8],
                  strict=True, inc=True)
        t = p.dv(lambda e, tt=tt: e.tensor_mul(comb[:, tt, :], l2s, alpt),
                 waits=[t9], strict=True, inc=True)
        ct_toks.append(t)
        if tt == 0 and os.environ.get("MOE_DBG2"):
            dbgg = C.nc.dram_tensor("dbgg", [128, 48], F32,
                                    kind="ExternalOutput").ap()
            dt1 = p.dma("sync", dbgg[:, 0:16], gcol, waits=[t])
            dt2 = p.dma("sync", dbgg[:, 16:24], l2s, waits=[t])
            dt3 = p.dma("sync", dbgg[:, 24:32], eqs, waits=[t])
            dt4 = p.dma("sync", dbgg[:, 32:40], comb[:, 0, :], waits=[t])
        if os.environ.get("MOE_DBG2"):
            return
    prev_tr = []
    for tt in range(NB):
        t = p.pe(lambda e, tt=tt: e.transpose(
            PS[0:8, 3584:3712], comb[:, tt, :], idn),
            waits=[ct_toks[tt]] + prev_tr, inc=True)
        t2 = p.dv(lambda e, tt=tt: e.tensor_copy(
            comb8b[0:8, tt * 128:(tt + 1) * 128], PS[0:8, 3584:3712]),
            waits=[t], inc=True)
        prev_tr = [t2]
    tcb = prev_tr[-1]
    if os.environ.get("MOE_DBG"):
        dbg = C.nc.dram_tensor("dbg8", [8, T], F32, kind="ExternalOutput").ap()
        p.dma("sync", dbg, comb8, waits=[tcb])
    p.barrier()
    if STAGE < 4:
        return
    w1t = [None, None]
    w2t = [None, None]
    w1cons = [[], []]
    w2cons = [[], []]

    def load_w1(e_, ib):
        sl = ib % 2
        w1t[sl] = p.dma("gpsimd", W1B[sl],
                        fc1h[e_][:, ib * 128:(ib + 1) * 128].rearrange(
                            "(c p) m -> p c m", p=128), waits=w1cons[sl])

    def load_w2(e_, ht):
        sl = ht % 2
        w2t[sl] = p.dma("gpsimd", W2B[sl],
                        fc2h[e_][:, ht * 128:(ht + 1) * 128].rearrange(
                            "(c p) m -> p c m", p=128), waits=w2cons[sl])

    gel_toks = [None] * NB
    f1_user = [None, None, None]
    f2_user = [None] * 4
    stg_cons = [[], []]
    cb_tok = []
    acc_tok = None
    load_w1(0, 0)
    for e_ in range(NE):
        prev = cb_tok
        cb_tok = []
        for tc in range(4):
            t = p.pe(lambda e, e_=e_, tc=tc: e.matmul(
                C.bank(3)[:, 0:512], sel[0:8, e_ * 128:(e_ + 1) * 128],
                comb8b[0:8, tc * 512:(tc + 1) * 512], start=True, stop=True),
                waits=[tcb] + prev + cb_tok, inc=True)
            cb_tok.append(p.dv(lambda e, tc=tc: e.tensor_copy(
                CB[:, tc * 512:(tc + 1) * 512], C.bank(3)[:, 0:512]),
                waits=[t], inc=True))
        for ib in range(NB):
            sl = ib % 2
            mt = []
            for tc in range(4):
                bk = (ib * 4 + tc) % 3
                for db in range(NB):
                    tok = p.pe(lambda e, sl=sl, db=db, tc=tc, bk=bk: e.matmul(
                        C.bank(bk)[:, 0:512], W1B[sl][:, db, :],
                        xn_blocks(db)[:, tc * 512:(tc + 1) * 512],
                        start=(db == 0), stop=(db == NB - 1)),
                        waits=([w1t[sl], f1_user[bk]] if db == 0 else []),
                        inc=(db == NB - 1))
                f1_user[bk] = p.ac(lambda e, e_=e_, ib=ib, tc=tc, bk=bk: e.activation(
                    C.W[:, ib, tc * 512:(tc + 1) * 512], C.bank(bk)[:, 0:512],
                    AF.Gelu, bias=fc1b[:, e_ * 16 + ib:e_ * 16 + ib + 1]),
                    waits=[tok], inc=True)
                mt.append(tok)
            gel_toks[ib] = f1_user[(ib * 4 + 3) % 3]
            w1cons[sl] = mt
            if ib + 1 < NB:
                load_w1(e_, ib + 1)
        if STAGE < 5:
            continue
        load_w2(e_, 0)
        hw = [gel_toks[NB - 1]]
        for ht in range(NB):
            sl = ht % 2
            stg = STG[sl]
            mt = []
            lastmm = {}
            for ib in range(NB):
                for tc in range(4):
                    w = []
                    if ib == 0:
                        w = [w2t[sl], f2_user[tc]] + hw
                    stop = (ib == NB - 1) and (e_ != 0)
                    lastmm[tc] = p.pe(lambda e, sl=sl, ib=ib, tc=tc, stop=stop:
                                      e.matmul(
                        C.bank(4 + tc)[:, 0:512], W2B[sl][:, ib, :],
                        C.W[:, ib, tc * 512:(tc + 1) * 512],
                        start=(ib == 0), stop=stop),
                        waits=w, inc=stop)
                    mt.append(lastmm[tc])
            if e_ == 0:
                for tc in range(4):
                    lastmm[tc] = p.pe(lambda e, tc=tc, ht=ht: e.matmul(
                        C.bank(4 + tc)[:, 0:512],
                        sfc2b[0:8, ht * 128:(ht + 1) * 128],
                        comb8b[0:8, tc * 512:(tc + 1) * 512],
                        start=False, stop=True), inc=True)
            w2cons[sl] = mt
            if ht + 1 < NB:
                load_w2(e_, ht + 1)
            elif e_ + 1 < NE:
                load_w1(e_ + 1, 0)
            evs = []
            for tc in range(4):
                f2_user[tc] = p.dv(lambda e, tc=tc, stg=stg: e.tensor_mul(
                    stg[:, tc * 512:(tc + 1) * 512], C.bank(4 + tc)[:, 0:512],
                    CB[:, tc * 512:(tc + 1) * 512]),
                    waits=[lastmm[tc], cb_tok[tc]] + stg_cons[sl], inc=True)
                evs.append(f2_user[tc])
            stg_cons[sl] = [p.dma(
                "gpsimd", g_out_hbm[ht * 128:(ht + 1) * 128, :], stg,
                waits=evs, accum_op=(ALU.bypass if e_ == 0 else ALU.add))]
    p.barrier()


def epi_phase(C, x_src, s_src, g_src, dst_hbm, stats=False):
    p, PS = C.p, C.PS
    cons = [[], [], [], [], [], []]
    sqcons = []
    fsq = C.fslot(6)
    petok = None
    for db in range(NB):
        sl = db % 2
        fx, fs, fg = C.fslot(0 + sl), C.fslot(2 + sl), C.fslot(4 + sl)
        r = db * 128
        lx = p.dma("sync", fx, x_src[r:r + 128, :], waits=cons[0 + sl])
        ls = p.dma("sync", fs, s_src[r:r + 128, :], waits=cons[2 + sl])
        lg = p.dma("sync", fg, g_src[r:r + 128, :], waits=cons[4 + sl])
        t1 = p.ac(lambda e, fg=fg: e.activation(fg, fg, AF.Silu),
                  waits=[lg], inc=True)
        t2 = p.dv(lambda e, fs=fs, fg=fg: e.tensor_mul(fs, fs, fg),
                  waits=[ls, t1], inc=True)
        t3 = p.dv(lambda e, fx=fx, fs=fs: e.tensor_add(fx, fx, fs),
                  waits=[lx], inc=True)
        st = p.dma("sync", dst_hbm[r:r + 128, :], fx, waits=[t3])
        cons[2 + sl] = [t3]
        cons[4 + sl] = [t2]
        if stats:
            tsq = p.ac(lambda e, fx=fx: e.activation(fsq, fx, AF.Square),
                       waits=[t3] + sqcons, inc=True)
            for tc in range(4):
                p.pe(lambda e, fx=fx, tc=tc, db=db: e.matmul(
                    PS[0:1, tc * 512:(tc + 1) * 512], C.onesf,
                    fx[:, tc * 512:(tc + 1) * 512],
                    start=(db == 0), stop=(db == NB - 1)),
                    waits=[C.const_tok, t3])
            for tc in range(4):
                petok = p.pe(lambda e, tc=tc, db=db: e.matmul(
                    PS[0:1, 2048 + tc * 512: 2048 + (tc + 1) * 512], C.onesf,
                    fsq[:, tc * 512:(tc + 1) * 512],
                    start=(db == 0), stop=(db == NB - 1)),
                    waits=[tsq], inc=(tc == 3))
            sqcons = [petok]
            cons[0 + sl] = [st, petok]
        else:
            cons[0 + sl] = [st]
    p.barrier()


def loadR_phase(C, src_hbm_bf16):
    t = C.p.dma("sync", C.R, src_hbm_bf16.rearrange("(c p) t -> p c t", p=128))
    C.p.barrier()
    return t


def rwkv1_phase(C, s_out_hbm):
    p, PS = C.p, C.PS
    Ff = C.F.rearrange("p a b -> p (a b)")
    FB = Ff.bitcast(BF16)
    TC2 = 1024
    sm = C.smalls
    xn1f = C.tn["xn1f"]

    import os
    RDBG = os.environ.get("RWKV_DBG", "")
    ln_phase(C, xn1f, dst_sbuf=lambda db: C.R[:, db, :],
             affine=(sm["ln1w"], sm["ln1b"]))
    p.barrier()
    if RDBG:
        dbgr = C.nc.dram_tensor("dbgr", [D, T], BF16, kind="ExternalOutput").ap()
    if RDBG == "z":
        p.dma("sync", dbgr.rearrange("(c p) t -> p c t", p=128), C.R)
        p.barrier()
        return
    t0 = p.dv(lambda e: e.tensor_scalar_mul(sm["negw"], sm["tmw"], -1.0),
              inc=True)
    tsh = p.dv(lambda e: e.tensor_copy(sm["shiftz"], C.R[:, :, 1023:1024]),
               inc=True)
    p.barrier()

    KIN = C.W[:, :, 0:1024]
    VIN = C.W[:, :, 1024:2048]
    RIN = FB[:, 0:16384].rearrange("p (a b) -> p a b", b=1024)
    WRG = [FB[:, 16384 + i * 2048:16384 + (i + 1) * 2048].rearrange(
        "p (a b) -> p a b", b=128) for i in range(2)]
    S3 = Ff[:, 10240:11264]
    S0 = Ff[:, 11264:12288]
    S1 = Ff[:, 12288:13312]
    S2 = Ff[:, 13312:14336]
    DIF = FB[:, 24576:26624]
    CTMP = Ff[:, 13312:13313]

    def mixes(c, zsrc, mk_list, outs, strict_first):
        toks = []
        d_m = DIF[:, 0:TC2 - 1]
        ctmp = CTMP
        for db in range(NB):
            zc = zsrc(db)[:, c * TC2:(c + 1) * TC2]
            zs_main = zsrc(db)[:, c * TC2 - 1 + 1:(c + 1) * TC2 - 1] if c == 0 \
                else zsrc(db)[:, c * TC2:(c + 1) * TC2 - 1]
            zs_main = zsrc(db)[:, c * TC2:(c + 1) * TC2 - 1]
            td = p.dv(lambda e, zc=zc, zs=zs_main, d_m=d_m: e.tensor_sub(
                d_m, zc[:, 1:], zs), waits=strict_first, inc=True)
            tm2a = None
            if c == 1:
                tm2a = p.dv(lambda e, zc=zc, db=db: e.tensor_sub(
                    ctmp, zc[:, 0:1], sm["shiftz"][:, db:db + 1]), inc=True)
            for (mk, out, sig) in mk_list:
                o = outs(out, db)
                tm1 = p.dv(lambda e, d_m=d_m, zs=zs_main, mk=mk, o=o, db=db:
                           e.scalar_tensor_tensor(
                    o[:, 1:], d_m, mk[:, db:db + 1], zs, ALU.mult, ALU.add),
                    waits=[td], strict=True, inc=True)
                if c == 0:
                    tm2 = p.dv(lambda e, zc=zc, mk=mk, o=o, db=db:
                               e.tensor_single_scalar(
                        o[:, 0:1], zc[:, 0:1], mk[:, db:db + 1], ALU.mult),
                        inc=True)
                else:
                    tm2 = p.dv(lambda e, mk=mk, o=o, db=db:
                               e.scalar_tensor_tensor(
                        o[:, 0:1], ctmp, mk[:, db:db + 1],
                        sm["shiftz"][:, db:db + 1], ALU.mult, ALU.add),
                        waits=[tm2a], strict=True, inc=True)
                tk = [tm1, tm2]
                if sig:
                    tk = [p.ac(lambda e, o=o: e.activation(o, o, AF.Sigmoid),
                               waits=tk, inc=True)]
                toks.extend(tk)
        return toks

    wring_cons = [[], []]
    wr_tok = {}

    def load_w(name, ob, j):
        wh = C.tn[name]
        ridx = load_w.ctr % 2
        load_w.ctr += 1
        wr_tok[j] = (ridx, p.dma("gpsimd", WRG[ridx],
                                 wh[:, ob * 128:(ob + 1) * 128].rearrange(
                                     "(c p) m -> p c m", p=128),
                                 waits=wring_cons[ridx]))
    load_w.ctr = 0

    pb_user = [None] * 8
    carry_saved = []
    for c in range(2):
        mix_toks = mixes(
            c, lambda db: C.R[:, db, :],
            [(sm["tmmixk"], "k", False), (sm["tmmixv"], "v", False),
             (sm["tmmixr"], "r", True)],
            lambda out, db: {"k": KIN[:, db, :], "v": VIN[:, db, :],
                             "r": RIN[:, db, :]}[out],
            [tsh, t0] if c == 0 else [])
        mix_toks = mix_toks[-1:] if False else mix_toks
        for ob in range(NB):
            rb = 4 + 2 * (ob % 2)
            mmt = {}
            for j, buf, b0 in ((0, KIN, 0), (1, VIN, 2), (2, RIN, rb)):
                load_w(("tmwk", "tmwv", "tmwr")[j], ob, j)
                ridx, ldt = wr_tok[j]
                for half in range(2):
                    for db in range(NB):
                        w = []
                        if db == 0:
                            w = [ldt, pb_user[b0 + half]] + (
                                mix_toks if ob == 0 else [])
                        tok = p.pe(lambda e, ridx=ridx, buf=buf, half=half,
                                   db=db, b0=b0, c=c: e.matmul(
                            C.bank(b0 + half)[:, 0:512],
                            WRG[ridx][:, db, :],
                            buf[:, db, half * 512:(half + 1) * 512],
                            start=(db == 0), stop=(db == NB - 1)),
                            waits=w, inc=(db == NB - 1))
                    mmt[(j, half)] = tok
                wring_cons[ridx] = [mmt[(j, 0)], mmt[(j, 1)]]
            kps = PS[:, 0:1024]
            vps = PS[:, 1024:2048]
            rps = PS[:, rb * 512:(rb + 2) * 512]
            t1 = p.dv(lambda e, ob=ob: e.tensor_scalar(
                S0, kps, sm["tmu"][:, ob:ob + 1], None, ALU.add),
                waits=[mmt[(0, 0)], mmt[(0, 1)]], inc=True)
            pb_user[0] = pb_user[1] = t1
            t2 = p.dv(lambda e, ob=ob: e.tensor_scalar(
                S2, S0, sm["tmw"][:, ob:ob + 1], 0.0, ALU.subtract, ALU.min),
                waits=[t1], strict=True, inc=True)
            t2b = p.dv(lambda e, ob=ob: e.tensor_single_scalar(
                S0, S0, sm["tmw"][:, ob:ob + 1], ALU.max),
                waits=[t2], strict=True, inc=True)
            te1 = p.ac(lambda e, ob=ob: e.activation(
                S1, S0, AF.Exp, scale=-1.0, bias=sm["negw"][:, ob:ob + 1]),
                waits=[t2b], inc=True)
            te2 = p.ac(lambda e: e.activation(S2, S2, AF.Exp),
                       waits=[t2], inc=True)
            tb = p.dv(lambda e, c=c, ob=ob: e.tensor_tensor_scan(
                S0, S1, S2, 0.0 if c == 0 else sm["carryb"][:, ob:ob + 1],
                ALU.mult, ALU.add),
                waits=[te1, te2], strict=True, inc=True)
            save_b = None
            if c == 0:
                save_b = p.dv(lambda e, ob=ob: e.tensor_copy(
                    sm["carryb"][:, ob:ob + 1], S0[:, 1023:1024]),
                    waits=[tb], strict=True, inc=True)
            t3 = p.dv(lambda e: e.tensor_mul(S2, S2, vps),
                      waits=[mmt[(1, 0)], mmt[(1, 1)], tb], strict=True,
                      inc=True)
            pb_user[2] = pb_user[3] = t3
            ta_ = p.dv(lambda e, c=c, ob=ob: e.tensor_tensor_scan(
                S3, S1, S2, 0.0 if c == 0 else sm["carrya"][:, ob:ob + 1],
                ALU.mult, ALU.add),
                waits=[t3], strict=True, inc=True)
            save_a = None
            if c == 0:
                save_a = p.dv(lambda e, ob=ob: e.tensor_copy(
                    sm["carrya"][:, ob:ob + 1], S3[:, 1023:1024]),
                    waits=[ta_], strict=True, inc=True)
            tln = p.ac(lambda e: e.activation(S0, S0, AF.Ln),
                       waits=[tb, save_b], inc=True)
            tex = p.ac(lambda e: e.activation(S0, S0, AF.Exp, scale=-1.0),
                       inc=True)
            tw = p.dv(lambda e: e.tensor_mul(S2, S3, S0),
                      waits=[ta_, tex, save_a], strict=True, inc=True)
            pb_user[2] = pb_user[3] = tw
            twr = p.dv(lambda e, c=c, ob=ob, rps=rps: e.tensor_mul(
                C.R[:, ob, c * TC2:(c + 1) * TC2], S2, rps),
                waits=[tw, mmt[(2, 0)], mmt[(2, 1)]], strict=True, inc=True)
            pb_user[rb] = pb_user[rb + 1] = twr
    p.barrier()
    if RDBG == "wr":
        p.dma("sync", dbgr.rearrange("(c p) t -> p c t", p=128), C.R)
        p.barrier()
        return

    xinsc = C.tn["xinsc"]
    WOR = [FB[:, 0:2048].rearrange("p (a b) -> p a b", b=128),
           FB[:, 2048:4096].rearrange("p (a b) -> p a b", b=128)]
    XT = [Ff[:, 2048:4096], Ff[:, 4096:6144]]
    AT = [Ff[:, 6144:8192], Ff[:, 8192:10240]]
    wo_cons = [[], []]
    wo_tok = [None, None]

    def load_wo2(ot):
        slw = ot % 2
        wo_tok[slw] = p.dma("gpsimd", WOR[slw],
                            C.tn["tmwo"][:, ot * 128:(ot + 1) * 128].rearrange(
                                "(c p) m -> p c m", p=128), waits=wo_cons[slw])

    load_wo2(0)
    xld = [None, None]
    atok = {}
    st_cons = [[], []]
    SQC = Ff[:, 10240:12288]
    S1AC = Ff[0:1, 12288:14336]
    S2AC = Ff[32:33, 12288:14336]
    sqcons = []
    dvacc = None
    petok = None
    for ot in range(NB):
        slw = ot % 2
        if ot + 1 < NB:
            load_wo2(ot + 1)
        xld[slw] = p.dma("sync", XT[slw], xn1f[ot * 128:(ot + 1) * 128, :],
                         waits=st_cons[slw])
        mmts = []
        for tc in range(4):
            bk = tc % 2
            for os_ in range(NB):
                w = []
                if os_ == 0:
                    w = [wo_tok[slw], atok.get(bk)]
                tok = p.pe(lambda e, slw=slw, os_=os_, tc=tc, bk=bk: e.matmul(
                    C.bank(bk)[:, 0:512], WOR[slw][:, os_, :],
                    C.R[:, os_, tc * 512:(tc + 1) * 512],
                    start=(os_ == 0), stop=(os_ == NB - 1)),
                    waits=w, inc=(os_ == NB - 1))
            mmts.append(tok)
            t1 = p.dv(lambda e, slw=slw, tc=tc, bk=bk, ot=ot: e.tensor_scalar(
                AT[slw][:, tc * 512:(tc + 1) * 512], C.bank(bk)[:, 0:512],
                sm["tmbo"][:, ot:ot + 1], None, ALU.add),
                waits=[tok] + st_cons[slw], inc=True)
            atok[bk] = p.dv(lambda e, slw=slw, tc=tc: e.tensor_add(
                AT[slw][:, tc * 512:(tc + 1) * 512],
                AT[slw][:, tc * 512:(tc + 1) * 512],
                XT[slw][:, tc * 512:(tc + 1) * 512]),
                waits=[xld[slw], t1], strict=True, inc=True)
        wo_cons[slw] = mmts
        tsq = p.ac(lambda e, slw=slw: e.activation(SQC, AT[slw], AF.Square),
                   waits=[atok[0], atok[1]] + sqcons, inc=True)
        for tc in range(4):
            bks = 2 + tc % 2
            m1 = p.pe(lambda e, slw=slw, tc=tc, bks=bks: e.matmul(
                C.bank(bks)[0:1, 0:512], C.onesf,
                AT[slw][:, tc * 512:(tc + 1) * 512],
                start=True, stop=True),
                waits=[C.const_tok, atok[0], atok[1], dvacc], inc=True)
            m2 = p.pe(lambda e, tc=tc, bks=bks: e.matmul(
                C.bank(bks)[32:33, 0:512], C.onesf,
                SQC[:, tc * 512:(tc + 1) * 512],
                start=True, stop=True), waits=[tsq], inc=True)
            if ot == 0:
                dvacc = p.dv(lambda e, tc=tc, bks=bks: e.tensor_copy(
                    S1AC[:, tc * 512:(tc + 1) * 512], C.bank(bks)[0:1, 0:512]),
                    waits=[m1], inc=True)
                dvacc = p.dv(lambda e, tc=tc, bks=bks: e.tensor_copy(
                    S2AC[:, tc * 512:(tc + 1) * 512], C.bank(bks)[32:33, 0:512]),
                    waits=[m2], inc=True)
            else:
                dvacc = p.dv(lambda e, tc=tc, bks=bks: e.tensor_add(
                    S1AC[:, tc * 512:(tc + 1) * 512],
                    S1AC[:, tc * 512:(tc + 1) * 512],
                    C.bank(bks)[0:1, 0:512]), waits=[m1, dvacc], strict=True,
                    inc=True)
                dvacc = p.dv(lambda e, tc=tc, bks=bks: e.tensor_add(
                    S2AC[:, tc * 512:(tc + 1) * 512],
                    S2AC[:, tc * 512:(tc + 1) * 512],
                    C.bank(bks)[32:33, 0:512]), waits=[m2, dvacc], strict=True,
                    inc=True)
        petok = dvacc
        sqcons = [petok]
        st_cons[slw] = [p.dma("sync", xinsc[ot * 128:(ot + 1) * 128, :],
                              AT[slw], waits=[atok[0], atok[1], petok]),
                        petok]
    p.barrier()

    ln_phase(C, xinsc, dst_sbuf=lambda db: C.R[:, db, :],
             affine=(sm["ln2w"], sm["ln2b"]), skip_stats=True,
             stats_loc=(Ff[0:1, 12288:14336], Ff[32:33, 12288:14336]))
    p.barrier()
    tsh2 = p.dv(lambda e: e.tensor_copy(sm["shiftz"], C.R[:, :, 1023:1024]),
                inc=True)
    p.barrier()
    if RDBG == "z2":
        p.dma("sync", dbgr.rearrange("(c p) t -> p c t", p=128), C.R)
        p.barrier()
        return

    if RDBG == "xin":
        return
    KIN2 = FB[:, 0:16384].rearrange("p (a b) -> p a b", b=1024)
    WRG2 = [FB[:, 16384 + i * 2048:16384 + (i + 1) * 2048].rearrange(
        "p (a b) -> p a b", b=128) for i in range(2)]
    T1 = Ff[:, 11264:12288]
    T2 = Ff[:, 12288:13312]
    ck_cons = [[], []]
    ck_tok = [None, None]

    def load_cmwk(ob):
        slw = ob % 2
        ck_tok[slw] = p.dma("gpsimd", WRG2[slw],
                            C.tn["cmwk"][:, ob * 128:(ob + 1) * 128].rearrange(
                                "(c p) m -> p c m", p=128), waits=ck_cons[slw])

    for c in range(2):
        mix_toks = mixes(
            c, lambda db: C.R[:, db, :],
            [(sm["cmmixk"], "k", False), (sm["cmmixr"], "r", False)],
            lambda out, db: {"k": KIN2[:, db, :],
                             "r": C.W[:, db, c * TC2:(c + 1) * TC2]}[out],
            [tsh2] if c == 0 else [])
        load_cmwk(0)
        for ob in range(NB):
            slw = ob % 2
            if ob + 1 < NB:
                load_cmwk(ob + 1)
            mmts = []
            for half in range(2):
                bk = half
                for db in range(NB):
                    w = []
                    if db == 0:
                        w = [ck_tok[slw], pb_user[bk]] +                             (mix_toks if ob == 0 else [])
                    tok = p.pe(lambda e, slw=slw, half=half, db=db, bk=bk:
                               e.matmul(
                        C.bank(bk)[:, 0:512], WRG2[slw][:, db, :],
                        KIN2[:, db, half * 512:(half + 1) * 512],
                        start=(db == 0), stop=(db == NB - 1)),
                        waits=w, inc=(db == NB - 1))
                mmts.append(tok)
            ck_cons[slw] = mmts
            t1 = p.dv(lambda e: e.tensor_scalar_max(T1, PS[:, 0:1024], 0.0),
                      waits=mmts, inc=True)
            pb_user[0] = pb_user[1] = t1
            t2 = p.dv(lambda e, c=c, ob=ob: e.tensor_mul(
                C.R[:, ob, c * TC2:(c + 1) * TC2], T1, T1),
                waits=[t1], strict=True, inc=True)
    p.barrier()
    if RDBG == "k2":
        p.dma("sync", dbgr.rearrange("(c p) t -> p c t", p=128), C.R)
        p.barrier()
        return

    WVR = [FB[:, 0:2048].rearrange("p (a b) -> p a b", b=128),
           FB[:, 2048:4096].rearrange("p (a b) -> p a b", b=128)]
    WRR = [FB[:, 4096:6144].rearrange("p (a b) -> p a b", b=128),
           FB[:, 6144:8192].rearrange("p (a b) -> p a b", b=128)]
    XT2 = [Ff[:, 4096:6144], Ff[:, 6144:8192]]
    OT2 = [Ff[:, 8192:10240], Ff[:, 10240:12288]]
    R2B = FB[:, 24576:25600]
    wv_cons = [[], []]
    wv_tok = [None, None]
    wq_cons = [[], []]
    wq_tok = [None, None]

    def load_cmwv(ot):
        slw = ot % 2
        wv_tok[slw] = p.dma("gpsimd", WVR[slw],
                            C.tn["cmwv"][:, ot * 128:(ot + 1) * 128].rearrange(
                                "(c p) m -> p c m", p=128), waits=wv_cons[slw])

    def load_cmwr(ot):
        slw = ot % 2
        wq_tok[slw] = p.dma("gpsimd", WRR[slw],
                            C.tn["cmwr"][:, ot * 128:(ot + 1) * 128].rearrange(
                                "(c p) m -> p c m", p=128), waits=wq_cons[slw])

    load_cmwv(0)
    load_cmwr(0)
    xld = [None, None]
    st2_cons = [[], []]
    r2tok = None
    last_t1 = None
    ot2tok = {}
    for ot in range(NB):
        slw = ot % 2
        if ot + 1 < NB:
            load_cmwv(ot + 1)
            load_cmwr(ot + 1)
        xld[slw] = p.dma("sync", XT2[slw], xinsc[ot * 128:(ot + 1) * 128, :],
                         waits=st2_cons[slw])
        mmv = []
        mmr = []
        for tc in range(4):
            bk = tc % 2
            bk2 = 2 + tc % 2
            for os_ in range(NB):
                wv_ = [wv_tok[slw], ot2tok.get(bk)] if os_ == 0 else []
                tokv = p.pe(lambda e, slw=slw, os_=os_, tc=tc, bk=bk: e.matmul(
                    C.bank(bk)[:, 0:512], WVR[slw][:, os_, :],
                    C.R[:, os_, tc * 512:(tc + 1) * 512],
                    start=(os_ == 0), stop=(os_ == NB - 1)),
                    waits=wv_, inc=(os_ == NB - 1))
                wr_ = [wq_tok[slw], r2tok] if os_ == 0 else []
                tokr = p.pe(lambda e, slw=slw, os_=os_, tc=tc, bk2=bk2:
                            e.matmul(
                    C.bank(bk2)[:, 0:512], WRR[slw][:, os_, :],
                    C.W[:, os_, tc * 512:(tc + 1) * 512],
                    start=(os_ == 0), stop=(os_ == NB - 1)),
                    waits=wr_, inc=(os_ == NB - 1))
            mmv.append(tokv)
            mmr.append(tokr)
            r2tok = p.ac(lambda e, tc=tc, bk2=bk2: e.activation(
                R2B[:, 0:512], C.bank(bk2)[:, 0:512], AF.Sigmoid),
                waits=[tokr, last_t1], inc=True)
            t1 = p.dv(lambda e, slw=slw, tc=tc, bk=bk: e.tensor_mul(
                OT2[slw][:, tc * 512:(tc + 1) * 512], C.bank(bk)[:, 0:512],
                R2B[:, 0:512]),
                waits=[tokv, r2tok] + st2_cons[slw], strict=True, inc=True)
            last_t1 = t1
            ot2tok[bk] = p.dv(lambda e, slw=slw, tc=tc: e.tensor_add(
                OT2[slw][:, tc * 512:(tc + 1) * 512],
                OT2[slw][:, tc * 512:(tc + 1) * 512],
                XT2[slw][:, tc * 512:(tc + 1) * 512]),
                waits=[xld[slw]], inc=True)
        wv_cons[slw] = mmv
        wq_cons[slw] = mmr
        st2_cons[slw] = [p.dma("sync", s_out_hbm[ot * 128:(ot + 1) * 128, :],
                               OT2[slw], waits=[ot2tok[0], ot2tok[1]])]
    p.barrier()


def ffn1_phase(C, base_out_hbm):
    import os
    FDBG = os.environ.get("FFN_DBG", "")
    p, PS = C.p, C.PS
    nc = C.nc
    Ff = C.F.rearrange("p a b -> p (a b)")
    FB = Ff.bitcast(BF16)
    sm = C.smalls
    zgd = nc.dram_tensor("zgd", [128, T], BF16, kind="Internal").ap()
    zud = nc.dram_tensor("zud", [128, T], BF16, kind="Internal").ap()
    zdsd = nc.dram_tensor("zdsd", [128, T], BF16, kind="Internal").ap()
    we8d = nc.dram_tensor("we8d", [8, T], F32, kind="Internal").ap()
    ZGW = FB[:, 8192:14336].rearrange("p (a b) -> p a b", b=T)
    ZUW = FB[:, 14336:20480].rearrange("p (a b) -> p a b", b=T)
    SL6 = Ff[:, 12288:14336]
    TG = SL6[:, 0:512]
    SG = SL6[:, 512:1024]
    TU = SL6[:, 1024:1536]
    HE = SL6[:, 0:512].bitcast(BF16)[:, 0:512]
    GATB = SL6[:, 1536:1792].bitcast(BF16)[:, 0:512]
    UPB = SL6[:, 1792:2048].bitcast(BF16)[:, 0:512]
    ZST = SL6[:, 1024:1536].bitcast(BF16)[:, 0:1024]
    H0 = C.W.rearrange("p a b -> p (a b)")[:, 0:22528].rearrange(
        "p (a b) -> p a b", b=512)
    WDQ2 = [C.W.rearrange("p a b -> p (a b)")[:, 22528:28160].rearrange(
        "p (a b) -> p a b", b=128),
        FB[:, 0:5632].rearrange("p (a b) -> p a b", b=128)]

    ZGB = SL6.bitcast(BF16)[:, 0:2048]
    WGR = [FB[:, i * 2048:(i + 1) * 2048].rearrange("p (a b) -> p a b", b=128)
           for i in range(2)]
    WUR = [FB[:, 4096 + i * 2048:4096 + (i + 1) * 2048].rearrange(
        "p (a b) -> p a b", b=128) for i in range(2)]
    tag = p.dma("gpsimd", WGR[0],
                C.tn["agT"].rearrange("(c p) m -> p c m", p=128))
    tau = p.dma("gpsimd", WUR[0],
                C.tn["auT"].rearrange("(c p) m -> p c m", p=128))
    for name, lhb, lt, dramt, dst16 in (("zg", WGR[0], tag, zgd, None),
                                        ("zu", WUR[0], tau, zud, None)):
        evt = []
        for tc in range(4):
            for db in range(NB):
                tok = p.pe(lambda e, db=db, tc=tc, lhb=lhb: e.matmul(
                    C.bank(tc)[:, 0:512], lhb[:, db, :],
                    C.R[:, db, tc * 512:(tc + 1) * 512],
                    start=(db == 0), stop=(db == NB - 1)),
                    waits=(C.smalls_tok + [lt]) if (db == 0) else [],
                    inc=(db == NB - 1))
            evt.append(p.dv(lambda e, tc=tc: e.tensor_scalar_mul(
                ZGB[:, tc * 512:(tc + 1) * 512], C.bank(tc)[:, 0:512], 2.0),
                waits=[tok], inc=True))
        t = p.dma("sync", dramt, ZGB, waits=evt)
        dstw = ZGW if name == "zg" else ZUW
        for e_ in range(NE):
            rr = (e_ % 3) * 32
            eb = e_ // 3
            p.dma("sync", dstw[rr:rr + 16, eb, :],
                  dramt[e_ * 16:(e_ + 1) * 16, :], waits=[t])
        p.barrier()
    WST = SL6[:, 0:128].rearrange("p (a b) -> p a b", b=8)
    gcol = SL6[:, 128:144]
    l2s = SL6[:, 144:152]
    eqs = SL6[:, 152:160]
    sr8 = sm["sr8f"].rearrange("p (c m) -> p c m", m=8)
    xn1fh = C.tn["xn1f"]
    xfr2 = [Ff[:, 0:2048], Ff[:, 2048:4096]]
    logacc2 = SL6[:, 160:288]
    rcons = [[], []]
    gacc = None
    for db in range(NB):
        sl = db % 2
        ld = p.dma("sync", xfr2[sl], xn1fh[db * 128:(db + 1) * 128, :],
                   waits=rcons[sl])
        for tt in range(NB):
            tok = p.pe(lambda e, tt=tt, db=db, sl=sl: e.matmul(
                C.bank(0)[:, tt * 8:(tt + 1) * 8],
                xfr2[sl][:, tt * 128:(tt + 1) * 128], sr8[:, db, :],
                start=True, stop=True),
                waits=[ld, gacc], inc=(tt == NB - 1))
        if db == 0:
            gacc = p.dv(lambda e: e.tensor_copy(logacc2, C.bank(0)[:, 0:128]),
                        waits=[tok], inc=True)
        else:
            gacc = p.dv(lambda e: e.tensor_add(logacc2, logacc2,
                                               C.bank(0)[:, 0:128]),
                        waits=[tok, gacc], strict=True, inc=True)
        rcons[sl] = [tok]
    lg = [gacc] * NB
    m1, m2, nm1, den = (gcol[:, 0:1], gcol[:, 1:2], gcol[:, 2:3], gcol[:, 3:4])
    wtr = []
    for tt in range(NB):
        L = logacc2[:, tt * 8:(tt + 1) * 8]
        t0 = p.dv(lambda e, L=L: e.reduce_max(m1, L, axis=AX.X),
                  waits=[lg[NB - 1]], inc=True)
        t1 = p.dv(lambda e, L=L: e.tensor_single_scalar(eqs, L, m1, ALU.is_ge),
                  waits=[t0], strict=True, inc=True)
        t2 = p.dv(lambda e, L=L: e.scalar_tensor_tensor(l2s, eqs, -1e30, L,
                                                        ALU.mult, ALU.add),
                  waits=[t1], strict=True, inc=True)
        t3 = p.dv(lambda e: e.reduce_max(m2, l2s, axis=AX.X),
                  waits=[t2], strict=True, inc=True)
        t4 = p.dv(lambda e: e.tensor_scalar_mul(nm1, m1, -1.0),
                  waits=[t3], strict=True, inc=True)
        ta = p.ac(lambda e, L=L: e.activation(l2s, L, AF.Exp, bias=nm1),
                  waits=[t4], inc=True)
        t5 = p.dv(lambda e: e.reduce_sum(den, l2s, axis=AX.X),
                  waits=[ta], strict=True, inc=True)
        t6 = p.dv(lambda e: e.reciprocal(den, den), waits=[t5], strict=True,
                  inc=True)
        t7 = p.dv(lambda e, L=L: e.tensor_single_scalar(eqs, L, m2, ALU.is_ge),
                  waits=[t6], strict=True, inc=True)
        t8 = p.dv(lambda e: e.tensor_mul(l2s, l2s, eqs), waits=[t7],
                  strict=True, inc=True)
        t9 = p.dv(lambda e, tt=tt: e.tensor_scalar(
            WST[:, tt, :], l2s, den, 2.0, ALU.mult, ALU.mult),
            waits=[t8], strict=True, inc=True)
        wtr.append(t9)
    WE8STG = Ff[64:72, 10240:12288]
    prev = []
    for tt in range(NB):
        t = p.pe(lambda e, tt=tt: e.transpose(
            PS[0:8, 3584:3712], WST[:, tt, :], sm["idn"]),
            waits=[wtr[tt]] + prev, inc=True)
        prev = [p.dv(lambda e, tt=tt: e.tensor_copy(
            WE8STG[:, tt * 128:(tt + 1) * 128], PS[0:8, 3584:3712]),
            waits=[t], inc=True)]
    p.dma("sync", we8d, WE8STG, waits=prev)
    p.barrier()
    if FDBG == "router":
        dbgf = nc.dram_tensor("dbgf", [8, T], F32, kind="ExternalOutput").ap()
        p.dma("sync", dbgf, WE8STG)
        p.barrier()
        return

    bgq = sm["bgq"].rearrange("p (b f) -> p b f", b=3)
    buq = sm["buq"].rearrange("p (b f) -> p b f", b=3)
    bg_h = C.tn["bgts"].rearrange("p (b f) -> p b f", b=3)
    bu_h = C.tn["buts"].rearrange("p (b f) -> p b f", b=3)
    bgq_cons = []
    bgq_tok = [None]

    def load_bgq(q):
        p.dma("gpsimd", bgq, bg_h[:, :, q * 1408:(q + 1) * 1408],
              waits=bgq_cons)
        p.dma("gpsimd", buq, bu_h[:, :, q * 1408:(q + 1) * 1408],
              waits=bgq_cons)
        bgq_tok[0] = p.dma(
            "gpsimd", sm["adq"].rearrange("p (a er) -> p a er", er=NE * R),
            adT_h[q * 1408:(q + 1) * 1408].rearrange(
                "(fb p) e r -> p fb (e r)", p=128), waits=bgq_cons)
    adtb = sm["adq"].rearrange("p (a e r) -> p a e r", e=NE, r=R)
    adT_h = C.tn["adT"]
    wg_cons = [[], []]
    wu_cons = [[], []]
    wg_tok = [None, None]
    wu_tok = [None, None]
    wgT, wuT, wdT = C.tn["wgT"], C.tn["wuT"], C.tn["wdT"]

    def load_wg(fb):
        slw = fb % 2
        wg_tok[slw] = p.dma("gpsimd", WGR[slw],
                            wgT[:, fb * 128:(fb + 1) * 128].rearrange(
                                "(c p) m -> p c m", p=128), waits=wg_cons[slw])
        wu_tok[slw] = p.dma("gpsimd", WUR[slw],
                            wuT[:, fb * 128:(fb + 1) * 128].rearrange(
                                "(c p) m -> p c m", p=128), waits=wu_cons[slw])

    HE8 = C.W.rearrange("p a b -> p (a b)")[:, 22528:26624].rearrange(
        "p (a b) -> p a b", b=512)
    TGb = [SL6[:, 0:256].bitcast(BF16), SL6[:, 256:512].bitcast(BF16)]
    SGb = [SL6[:, 512:768].bitcast(BF16), SL6[:, 768:1024].bitcast(BF16)]
    TUb = [SL6[:, 1024:1280].bitcast(BF16), SL6[:, 1280:1536].bitcast(BF16)]
    bank_user = {}
    buf_user = {}
    b7_user = [None]
    zd_ev = []
    wd_cons = []
    wdt = [None]
    st_base = []
    for tci in range(4):
        load_wg(0)
        h0toks = []
        glt = ult = None
        pending_zd = []
        for fb in range(NFB):
            slw = fb % 2
            if fb % 11 == 0:
                bgq_cons = [glt, ult] + [buf_user.get("he8_" + str(z))
                                         for z in range(NE)]
                load_bgq(fb // 11)
            if fb + 1 < NFB:
                load_wg(fb + 1)
            for db in range(NB):
                w = ([wg_tok[slw], bank_user.get(0)] if db == 0 else [])
                tokg = p.pe(lambda e, slw=slw, db=db, tci=tci: e.matmul(
                    C.bank(0)[:, 0:512], WGR[slw][:, db, :],
                    C.R[:, db, tci * 512:(tci + 1) * 512],
                    start=(db == 0), stop=(db == NB - 1)),
                    waits=w, inc=(db == NB - 1))
            for db in range(NB):
                w = ([wu_tok[slw], bank_user.get(1)] if db == 0 else [])
                toku = p.pe(lambda e, slw=slw, db=db, tci=tci: e.matmul(
                    C.bank(1)[:, 0:512], WUR[slw][:, db, :],
                    C.R[:, db, tci * 512:(tci + 1) * 512],
                    start=(db == 0), stop=(db == NB - 1)),
                    waits=w, inc=(db == NB - 1))
            wg_cons[slw] = [tokg]
            wu_cons[slw] = [toku]
            tgb = p.ac(lambda e: e.copy(GATB, C.bank(0)[:, 0:512]),
                       waits=[tokg, buf_user.get("gatb")], inc=True)
            tub = p.ac(lambda e: e.copy(UPB, C.bank(1)[:, 0:512]),
                       waits=[toku, buf_user.get("upb")], inc=True)
            tsg = p.ac(lambda e, fb=fb: e.activation(
                H0[:, fb, :], C.bank(0)[:, 0:512], AF.Silu),
                waits=[tokg] + wd_cons, inc=True)
            th0 = p.dv(lambda e, fb=fb: e.tensor_mul(
                H0[:, fb, :], H0[:, fb, :], C.bank(1)[:, 0:512]),
                waits=[tsg, toku], strict=True, inc=True)
            h0toks.append(th0)
            bank_user[0] = tsg
            bank_user[1] = th0
            for (ze, zfb, zthe) in pending_zd:
                rr2 = (ze % 3) * 32
                bkz2 = 4 + ze // 3
                buf_user["he8_" + str(ze)] = p.pe(
                    lambda e, ze=ze, zfb=zfb, bkz2=bkz2, rr2=rr2: e.matmul(
                        C.bank(bkz2)[rr2:rr2 + 16, 0:512],
                        adtb[:, zfb % 11, ze, :], HE8[:, ze, :],
                        start=(zfb == 0), stop=(zfb == NFB - 1),
                        tile_position=(0, rr2)),
                    waits=[zthe] + (zd_ev if zfb == 0 else []), inc=True)
            pending_zd = []
            for e_ in range(NE):
                par = e_ % 2
                rr = (e_ % 3) * 32
                fq = (fb % 11) * 128
                eb = e_ // 3
                bg_ = 0 if par == 0 else 2
                bu_ = 1 if par == 0 else 3
                w = [bank_user.get(bg_), bgq_tok[0]]
                glt = p.pe(lambda e, e_=e_, fq=fq, eb=eb, tci=tci, rr=rr,
                           bg_=bg_: e.matmul(
                    C.bank(bg_)[:, 0:512], bgq[rr:rr + 16, eb, fq:fq + 128],
                    ZGW[rr:rr + 16, eb, tci * 512:(tci + 1) * 512],
                    start=True, stop=True), waits=w, inc=True)
                w = [bank_user.get(bu_)]
                ult = p.pe(lambda e, e_=e_, fq=fq, eb=eb, tci=tci, rr=rr,
                           bu_=bu_: e.matmul(
                    C.bank(bu_)[:, 0:512], buq[rr:rr + 16, eb, fq:fq + 128],
                    ZUW[rr:rr + 16, eb, tci * 512:(tci + 1) * 512],
                    start=True, stop=True), waits=w, inc=True)
                ttg = p.dv(lambda e, par=par, bg_=bg_: e.tensor_tensor(
                    TGb[par], C.bank(bg_)[:, 0:512], GATB, ALU.add),
                    waits=[glt, tgb, buf_user.get("he" + str(par))], inc=True)
                ttu = p.dv(lambda e, par=par, bu_=bu_: e.tensor_tensor(
                    TUb[par], C.bank(bu_)[:, 0:512], UPB, ALU.add),
                    waits=[ult, tub, buf_user.get("tu" + str(par))], inc=True)
                bank_user[bg_] = ttg
                bank_user[bu_] = ttu
                tsg2 = p.ac(lambda e, par=par: e.activation(
                    SGb[par], TGb[par], AF.Silu),
                    waits=[ttg, buf_user.get("sgb" + str(par))], inc=True)
                the = p.dv(lambda e, par=par, e_=e_: e.tensor_mul(
                    HE8[:, e_, :], SGb[par], TUb[par]),
                    waits=[tsg2, ttu, buf_user.get("he8_" + str(e_))],
                    strict=True, inc=True)
                buf_user["tu" + str(par)] = the
                buf_user["sgb" + str(par)] = the
                buf_user["he" + str(par)] = the
                pending_zd.append((e_, fb, the))
            buf_user["gatb"] = ttg
            buf_user["upb"] = ttu
        for (ze, zfb, zthe) in pending_zd:
            rr2 = (ze % 3) * 32
            bkz2 = 4 + ze // 3
            buf_user["he8_" + str(ze)] = p.pe(
                lambda e, ze=ze, zfb=zfb, bkz2=bkz2, rr2=rr2: e.matmul(
                    C.bank(bkz2)[rr2:rr2 + 16, 0:512],
                    adtb[:, zfb % 11, ze, :], HE8[:, ze, :],
                    start=(zfb == 0), stop=(zfb == NFB - 1),
                    tile_position=(0, rr2)),
                waits=[zthe] + (zd_ev if zfb == 0 else []), inc=True)
        pending_zd = []
        stg = [SL6[:, 0:512], SL6[:, 512:1024]]
        wd_cons2 = [wd_cons, wg_cons[0] + wg_cons[1] + wu_cons[0] + wu_cons[1]]
        wdt2 = [None, None]

        def load_wd(dt_):
            sb2 = dt_ % 2
            wdt2[sb2] = p.dma("gpsimd", WDQ2[sb2],
                              wdT[:, dt_ * 128:(dt_ + 1) * 128].rearrange(
                                  "(c p) m -> p c m", p=128),
                              waits=wd_cons2[sb2] +
                              ([buf_user.get("he8_" + str(z))
                                for z in range(NE)] if dt_ <= 1 else []))

        load_wd(0)
        for dt_ in range(NB):
            sb = dt_ % 2
            if dt_ + 1 < NB:
                load_wd(dt_ + 1)
            for fs in range(NFB):
                w = [wdt2[sb], b7_user[0]] if fs == 0 else []
                tok = p.pe(lambda e, fs=fs, sb=sb: e.matmul(
                    C.bank(7)[:, 0:512], WDQ2[sb][:, fs, :], H0[:, fs, :],
                    start=(fs == 0), stop=(fs == NFB - 1)),
                    waits=w + ([h0toks[-1]] if fs == 0 else []),
                    inc=(fs == NFB - 1))
            wd_cons2[sb] = [tok]
            wd_cons = [tok]
            b7_user[0] = p.dv(lambda e, sb=sb: e.tensor_copy(
                stg[sb], C.bank(7)[:, 0:512]),
                waits=[tok] + st_base[-2:-1], inc=True)
            st_base.append(p.dma(
                "sync", base_out_hbm[dt_ * 128:(dt_ + 1) * 128,
                                     tci * 512:(tci + 1) * 512],
                stg[sb], waits=[b7_user[0]]))
        wg_cons[0] = wg_cons[0] + wd_cons2[1]
        wu_cons[0] = wu_cons[0] + wd_cons2[1]
        zd_ev = []
        prev_tz2 = None
        for e_ in range(NE):
            rr = (e_ % 3) * 32
            bkz = 4 + e_ // 3
            tz = p.dv(lambda e, e_=e_, rr=rr, bkz=bkz: e.tensor_copy(
                ZST[0:16, 0:512], C.bank(bkz)[rr:rr + 16, 0:512]),
                waits=[prev_tz2], inc=True)
            prev_tz2 = p.dma("sync",
                             zdsd[e_ * 16:(e_ + 1) * 16,
                                  tci * 512:(tci + 1) * 512],
                             ZST[0:16, 0:512], waits=[tz])
            zd_ev.append(prev_tz2)
        p.barrier()
    if FDBG == "base":
        return
    WE8 = Ff[0:8, 4096:6144]
    ZDST = SL6[:, 0:1024].bitcast(BF16)[:, 0:2048]
    ACC4 = [Ff[:, 6144:8192], Ff[:, 8192:10240]]
    t = p.dma("sync", WE8, we8d)
    t2 = p.dma("sync", ZDST, zdsd)
    p.barrier()
    sc_toks = []
    for tc in range(4):
        tb = p.pe(lambda e, tc=tc: e.matmul(
            C.bank(0)[:, 0:512], sm["sel16r"], WE8[:, tc * 512:(tc + 1) * 512],
            start=True, stop=True), waits=sc_toks, inc=True)
        sc_toks.append(p.dv(lambda e, tc=tc: e.tensor_mul(
            ZDST[:, tc * 512:(tc + 1) * 512], ZDST[:, tc * 512:(tc + 1) * 512],
            C.bank(0)[:, 0:512]), waits=[tb], strict=True, inc=True))
    bdtb = sm["bgq"][:, 0:2048]
    tbd = p.dma("gpsimd", bdtb, C.tn["bdT"])
    acc_user = [None, None]
    st_acc = []
    for dt_ in range(NB):
        sb = dt_ % 2
        for tc in range(4):
            tok = p.pe(lambda e, dt_=dt_, tc=tc: e.matmul(
                C.bank(1 + tc % 2)[:, 0:512],
                bdtb[:, dt_ * 128:(dt_ + 1) * 128],
                ZDST[:, tc * 512:(tc + 1) * 512],
                start=True, stop=True),
                waits=sc_toks + [acc_user[tc % 2], tbd], inc=True)
            acc_user[tc % 2] = p.dv(lambda e, sb=sb, tc=tc: e.tensor_copy(
                ACC4[sb][:, tc * 512:(tc + 1) * 512],
                C.bank(1 + tc % 2)[:, 0:512]),
                waits=[tok] + st_acc[-2:-1], inc=True)
        st_acc.append(p.dma("gpsimd",
                            base_out_hbm[dt_ * 128:(dt_ + 1) * 128, :],
                            ACC4[sb], waits=[acc_user[0], acc_user[1],
                                             st_base[-1]],
                            accum_op=ALU.add))
    p.barrier()


PHASE_IO = {
    "ln0": (["xT"], ["xn0b", "xn0f"]),
    "attn0": (["xn0b"], ["s0sc"]),
    "moe0": (["xn0b", "xn0f"], ["g0sc"]),
    "epi0": (["xT", "s0sc", "g0sc"], ["x1sc"]),
    "ln1": (["x1sc"], ["xn1b", "xn1f"]),
    "rwkv1": (["xn1f"], ["s1sc", "xinsc"]),
    "ffn1": (["xn1b", "xn1f"], ["base1"]),
    "epi1": (["x1sc", "s1sc", "base1"], ["out"]),
}

SMALLS_SPEC = {
    "bout": ([128, 16], F32, "bout", False, "g"),
    "bqkv": ([128, 48], F32, "bqkv", False, "g"),
    "bqkvb": ([128, 48], BF16, "bqkv", True, "g"),
    "beff": ([128, 1], F32, None, False, "g"),
    "fc1b": ([128, 128], F32, "fc1b", False, "ph"),
    "fc2b": ([8, D], BF16, "fc2b", True, "ph"),
    "alpha": ([1, 8], F32, "alpha", False, "g"),
    "alphat": ([128, 8], F32, None, False, "g"),
    "comb8b": ([8, T], BF16, None, False, "ph"),
    "sel": ([8, 1024], BF16, "sel", True, "ph"),
    "idn": ([128, 128], F32, "identity", False, "g"),
    "ln1w": ([128, 16], F32, "ln1w", False, "g"),
    "ln1b": ([128, 16], F32, "ln1b", False, "g"),
    "ln2w": ([128, 16], F32, "ln2w", False, "g"),
    "ln2b": ([128, 16], F32, "ln2b", False, "g"),
    "tmu": ([128, 16], F32, "tmu", False, "g"),
    "tmw": ([128, 16], F32, "tmw", False, "g"),
    "negw": ([128, 16], F32, None, False, "g"),
    "tmmixk": ([128, 16], F32, "tmmixk", False, "g"),
    "tmmixv": ([128, 16], F32, "tmmixv", False, "g"),
    "tmmixr": ([128, 16], F32, "tmmixr", False, "g"),
    "tmbo": ([128, 16], F32, "tmbo", False, "g"),
    "cmmixk": ([128, 16], F32, "cmmixk", False, "g"),
    "cmmixr": ([128, 16], F32, "cmmixr", False, "g"),
    "shiftz": ([128, 16], F32, None, False, "g"),
    "carrya": ([128, 16], F32, None, False, "g"),
    "carryb": ([128, 16], F32, None, False, "g"),
    "sr8f": ([128, 16 * 8], F32, "wr8T", False, "g"),
    "gatewf": ([128, 16 * 8], F32, "gatew", False, "g"),
    "bgq": ([128, 3 * 1408], BF16, None, False, "ph"),
    "buq": ([128, 3 * 1408], BF16, None, False, "ph"),
    "adq": ([128, 11 * NE * R], BF16, None, False, "ph"),
    "sel16r": ([8, 128], F32, "sel16r", False, "g"),
}
SM_PHASE = {"fc1b": 0, "fc2b": 0, "comb8b": 0, "sel": 0,
            "bgq": 1, "buq": 1, "adq": 1}
SM_ARENA_F32 = 4928
PHASE_SMALLS = {
    "ln0": [], "epi0": [], "epi1": [], "ln1": [],
    "attn0": ["bout", "bqkv", "bqkvb", "beff"],
    "moe0": ["fc1b", "fc2b", "alpha", "alphat", "comb8b", "sel",
             "idn", "gatewf"],
    "rwkv1": ["ln1w", "ln1b", "ln2w", "ln2b", "tmu", "tmw", "negw", "tmmixk",
              "tmmixv", "tmmixr", "tmbo", "cmmixk", "cmmixr", "shiftz",
              "carrya", "carryb"],
    "ffn1": ["sr8f", "bgq", "buq", "adq", "sel16r", "idn"],
}
PHASE_CCOLS = {
    "ln0": [1e-5], "ln1": [1e-5], "attn0": [SCALE_QK],
    "moe0": [], "epi0": [], "epi1": [], "rwkv1": [1e-5], "ffn1": [],
}
EXTRA_IN = {"attn0": ["nrmdram_decl"], "moe0": [], }


def build_launch(phases):
    nc = bass.Bass("TRN2", target_bir_lowering=False)
    p = Prog(nc)
    needs = []
    sin, sout = [], []
    produced = set()
    for ph in phases:
        for n in PHASE_NEEDS.get(ph, []):
            if n not in needs:
                needs.append(n)
        pin, pout = PHASE_IO[ph]
        for n in pin:
            if n != "xT" and n not in produced and n not in sin:
                sin.append(n)
        for n in pout:
            produced.add(n)
            if n not in sout:
                sout.append(n)
        if "xT" in pin and "xT" not in needs:
            needs.append("xT")
    consumed_later = {"xinsc"}
    for ph in phases:
        for n in PHASE_IO[ph][0]:
            consumed_later.add(n)
    sout_d = {}
    for n in sout:
        if n == "out":
            sout_d[n] = "ExternalOutput"
        elif n in consumed_later and len(phases) > 1:
            sout_d[n] = "Internal"
        else:
            sout_d[n] = "ExternalOutput"
    C = Dev(nc, p, needs, sin, sout_d)
    if any(ph == "attn0" for ph in phases):
        C.tn["nrmdram"] = nc.dram_tensor("nrmdram", [1, NH * T], F32,
                                         kind="Internal").ap()
    ccols = sorted({v for ph in phases for v in PHASE_CCOLS[ph]})
    C.ccol_init(ccols)
    C.smalls_tok = []
    sneed = []
    for ph in phases:
        for sm in PHASE_SMALLS[ph]:
            if sm not in sneed:
                sneed.append(sm)
    SM_REARR = {
        "sr8f": (lambda a: a.rearrange("(c p) m -> p c m", p=128),
                 lambda d: d.rearrange("p (c m) -> p c m", m=8)),
        "gatewf": (lambda a: a.rearrange("(c p) m -> p c m", p=128),
                   lambda d: d.rearrange("p (c m) -> p c m", m=8)),
        "agTb": (lambda a: a.rearrange("(c p) m -> p c m", p=128),
                 lambda d: d.rearrange("p (c m) -> p c m", m=128)),
        "auTb": (lambda a: a.rearrange("(c p) m -> p c m", p=128),
                 lambda d: d.rearrange("p (c m) -> p c m", m=128)),
        "adTb": (lambda a: a.rearrange("(fb p) e r -> p fb (e r)", p=128),
                 lambda d: d.rearrange("p (fb er) -> p fb er", er=NE * R)),
    }
    C.sm_arena = nc.alloc_sbuf_tensor("smArena", [128, SM_ARENA_F32], F32).ap()
    sm_off = {}
    for sm in sneed:
        shape, dt, src, cast, grp = SMALLS_SPEC[sm]
        if grp == "ph":
            ph_id = SM_PHASE[sm]
            off = sm_off.get(ph_id, 0)
            nfp32 = (shape[1] * (2 if dt == BF16 else 4) + 31) // 32 * 8
            flat = C.sm_arena.rearrange("p a -> p (a)") if False else C.sm_arena
            view = flat[:, off:off + nfp32]
            if dt == BF16:
                view = view.bitcast(BF16)
            C.smalls[sm] = view[0:shape[0], 0:shape[1]]
            sm_off[ph_id] = off + nfp32
            assert sm_off[ph_id] <= SM_ARENA_F32, (sm, sm_off)
        else:
            C.smalls[sm] = nc.alloc_sbuf_tensor("sm_" + sm, shape, dt).ap()
        if src is not None:
            sap = C.tn[src]
            dap = C.smalls[sm]
            if sm in SM_REARR:
                fs, fd = SM_REARR[sm]
                sap = fs(sap)
                dap = fd(dap)
            C.smalls_tok.append(p.dma("gpsimd" if cast else "sync",
                                      dap, sap))
    p.barrier()
    Rb = lambda db: C.R[:, db, :]
    for ph in phases:
        if ph == "ln0":
            ln_phase(C, C.tn["xT"], dst_sbuf=Rb, out_bf16=C.tn["xn0b"],
                     out_f32=C.tn["xn0f"], fp32_stats=True)
        elif ph == "attn0":
            if phases[0] == ph or "ln0" not in phases:
                loadR_phase(C, C.tn["xn0b"])
            attn0_phase(C, Rb, C.tn["s0sc"])
        elif ph == "moe0":
            loadR_phase(C, C.tn["xn0b"])
            moe0_phase(C, Rb, C.tn["g0sc"])
        elif ph == "epi0":
            epi_phase(C, C.tn["xT"], C.tn["s0sc"], C.tn["g0sc"], C.tn["x1sc"],
                      stats=("ln1" in phases))
        elif ph == "ln1":
            ln_phase(C, C.tn["x1sc"], dst_sbuf=Rb, out_bf16=C.tn["xn1b"],
                     out_f32=C.tn["xn1f"], fp32_stats=True,
                     skip_stats=("epi0" in phases))
        elif ph == "rwkv1":
            rwkv1_phase(C, C.tn["s1sc"])
        elif ph == "ffn1":
            loadR_phase(C, C.tn["xn1b"])
            ffn1_phase(C, C.tn["base1"])
        elif ph == "epi1":
            epi_phase(C, C.tn["x1sc"], C.tn["s1sc"], C.tn["base1"], C.tn["out"])
        else:
            raise NotImplementedError(ph)
        p.barrier()
    p.finalize()
    innames = needs + sin
    souts = [n for n in sout if sout_d[n] == "ExternalOutput"]
    return nc, innames, souts


FUSED = True
if FUSED:
    LAUNCH_PLAN = [["ln0", "attn0", "moe0", "epi0", "ln1", "rwkv1", "ffn1",
                    "epi1"]]
else:
    LAUNCH_PLAN = [["ln0"], ["attn0"], ["moe0"], ["epi0"], ["ln1"], ["rwkv1"],
                   ["ffn1"], ["epi1"]]


def kernel(**inputs):
    h = prep_host(inputs)
    h["sel"] = SELMAT
    per_core_x = h.pop("_x_per_core")
    state = {}
    for phases in LAUNCH_PLAN:
        nc, innames, outnames = build_launch(phases)
        in_maps = []
        for c in range(8):
            m = {}
            for n in innames:
                if n == "xT":
                    m[n] = per_core_x[c]
                elif n in h:
                    m[n] = np.ascontiguousarray(h[n], dtype=None)
                else:
                    m[n] = state[n][c]
            in_maps.append(m)
        res = run_bass_kernel_spmd(nc, in_maps, core_ids=list(range(8)))
        for n in outnames:
            state[n] = [np.asarray(res.results[c][n]) for c in range(8)]
    out = np.stack([state["out"][c].T for c in range(8)])
    return np.ascontiguousarray(out, dtype=np.float32)


SELMAT = np.zeros((8, 1024), dtype=np.float32)
for _e in range(8):
    SELMAT[_e, _e * 128:(_e + 1) * 128] = 1.0
IN_SPECS["sel"] = ([8, 1024], F32)


# version 09
# speedup vs baseline: 1.1341x; 1.1341x over previous
import numpy as np
import ml_dtypes
import concourse.bass as bass
import concourse.mybir as mybir
from concourse.bass_utils import run_bass_kernel_spmd

F32 = mybir.dt.float32
BF16 = mybir.dt.bfloat16
AF = mybir.ActivationFunctionType
ALU = mybir.AluOpType
AX = mybir.AxisListType

D = 2048
T = 2048
NB = D // 128
NTC = T // 512
NH = 16
DH = 128
NE = 8
DFF = 5632
NFB = DFF // 128
R = 16

ENGINES = ("sync", "gpsimd", "tensor", "vector", "scalar")
ENG_SEM = {"tensor": "spe", "vector": "sdv", "scalar": "sac"}
DMA_SEMS = ("dsy", "dgp")


class Prog:

    def __init__(self, nc):
        self.nc = nc
        self.ops = {e: [] for e in ENGINES}
        self.counts = {}
        self.lastwait = {e: {} for e in ENGINES}
        self.handles = {}

    def _wait_filter(self, eng, waits, strict=False):
        ws = []
        for tok in waits:
            if tok is None:
                continue
            (s, v) = tok
            if ENG_SEM.get(eng) == s and not strict:
                continue
            if self.lastwait[eng].get(s, 0) >= v:
                continue
            self.lastwait[eng][s] = v
            ws.append((s, v))
        return ws

    def emit(self, eng, fn, waits=(), inc=None, inc_amt=1, strict=False):
        ws = self._wait_filter(eng, waits, strict)
        tok = None
        if inc is not None:
            self.counts[inc] = self.counts.get(inc, 0) + inc_amt
            tok = (inc, self.counts[inc])
        self.ops[eng].append([ws, fn, inc, inc_amt])
        return tok

    def dma(self, queue, dst, src, waits=(), **kw):
        sem = "dsy" if queue == "sync" else "dgp"
        return self.emit(queue, lambda e: e.dma_start(dst, src, **kw),
                         waits=waits, inc=sem, inc_amt=16)

    def pe(self, fn, waits=(), inc=False, strict=False):
        return self.emit("tensor", fn, waits, "spe" if inc else None,
                         strict=strict)

    def dv(self, fn, waits=(), inc=False, strict=False):
        return self.emit("vector", fn, waits, "sdv" if inc else None,
                         strict=strict)

    def ac(self, fn, waits=(), inc=False, strict=False):
        return self.emit("scalar", fn, waits, "sac" if inc else None,
                         strict=strict)

    def barrier(self):
        toks = []
        for eng, sem in ENG_SEM.items():
            lst = self.ops[eng]
            last = None
            for op in reversed(lst):
                if op[1] is not None:
                    last = op
                    break
            if last is not None:
                if last[2] is None:
                    last[2] = sem
                    last[3] = 1
                    self.counts[sem] = self.counts.get(sem, 0) + 1
                toks.append((sem, self.counts.get(sem, 0)))
        for s in DMA_SEMS:
            if self.counts.get(s, 0):
                toks.append((s, self.counts[s]))
        for eng in ENGINES:
            if self.ops[eng] or eng in ("sync", "gpsimd", "tensor", "vector", "scalar"):
                ws = self._wait_filter(eng, toks)
                if ws:
                    self.ops[eng].append([ws, None, None, 1])
        return toks

    def finalize(self):
        nc = self.nc
        import contextlib
        with contextlib.ExitStack() as es:
            for s in list(ENG_SEM.values()) + list(DMA_SEMS):
                self.handles[s] = es.enter_context(nc.semaphore(s))
            blk = es.enter_context(nc.Block())
            for eng in ENGINES:
                if not self.ops[eng]:
                    continue

                def body(e, eng=eng):
                    for ws, fn, inc, amt in self.ops[eng]:
                        for (s, v) in ws:
                            e.wait_ge(self.handles[s], v)
                        if fn is not None:
                            inst = fn(e)
                            if inc is not None:
                                inst.then_inc(self.handles[inc], amt)
                getattr(blk, eng)(body)


def col128(v):
    v = np.asarray(v)
    return np.ascontiguousarray(v.reshape(-1, 128).T)


def prep_host(inp):
    h = {}
    x = np.asarray(inp["L0_attn_x"])
    h["_x_per_core"] = [np.ascontiguousarray(x[b].T) for b in range(8)]
    h["wqkv"] = np.asarray(inp["L0_attn_W_qkv"])
    h["bqkv"] = col128(inp["L0_attn_b_qkv"])
    h["wout"] = np.asarray(inp["L0_attn_W_out"])
    h["bout"] = col128(inp["L0_attn_b_out"])
    h["gatew"] = np.asarray(inp["L0_ffn_gate_w"])
    h["fc1"] = np.asarray(inp["L0_ffn_fc1_w"])
    h["fc1b"] = np.stack([col128(b) for b in np.asarray(inp["L0_ffn_fc1_b"])])
    h["fc2"] = np.asarray(inp["L0_ffn_fc2_w"])
    h["fc2b"] = np.asarray(inp["L0_ffn_fc2_b"])
    h["alpha"] = np.asarray(inp["L0_ffn_alpha"]).reshape(1, 8)
    for k in ("ln1_w", "ln1_b", "ln2_w", "ln2_b", "tm_u", "tm_w", "tm_mix_k",
              "tm_mix_v", "tm_mix_r", "tm_bo", "cm_mix_k", "cm_mix_r"):
        h[k.replace("_", "")] = col128(inp[f"L1_attn_{k}"])
    for k in ("tm_Wk", "tm_Wv", "tm_Wr", "tm_Wo", "cm_Wk", "cm_Wv", "cm_Wr"):
        h[k.replace("_", "").lower()] = np.ascontiguousarray(np.asarray(inp[f"L1_attn_{k}"]).T)
    h["wgT"] = np.ascontiguousarray(np.asarray(inp["L1_ffn_Wg"]).T)
    h["wuT"] = np.ascontiguousarray(np.asarray(inp["L1_ffn_Wu"]).T)
    h["wdT"] = np.ascontiguousarray(np.asarray(inp["L1_ffn_Wd"]).T)
    h["wr8T"] = np.ascontiguousarray(np.asarray(inp["L1_ffn_Wr"]).T)
    h["agT"] = np.ascontiguousarray(np.concatenate(
        [np.asarray(inp["L1_ffn_Ag"])[e].T for e in range(NE)], axis=1))
    h["auT"] = np.ascontiguousarray(np.concatenate(
        [np.asarray(inp["L1_ffn_Au"])[e].T for e in range(NE)], axis=1))
    h["bgT"] = np.ascontiguousarray(np.stack(
        [np.asarray(inp["L1_ffn_Bg"])[e].T for e in range(NE)], axis=1))
    h["buT"] = np.ascontiguousarray(np.stack(
        [np.asarray(inp["L1_ffn_Bu"])[e].T for e in range(NE)], axis=1))
    h["adT"] = np.ascontiguousarray(np.stack(
        [np.asarray(inp["L1_ffn_Ad"])[e].T for e in range(NE)], axis=1))
    h["bdT"] = np.ascontiguousarray(np.concatenate(
        [np.asarray(inp["L1_ffn_Bd"])[e].T for e in range(NE)], axis=0))
    bgts = np.zeros((128, 3 * DFF), dtype=np.float32)
    buts = np.zeros((128, 3 * DFF), dtype=np.float32)
    Bg = np.asarray(inp["L1_ffn_Bg"]); Bu = np.asarray(inp["L1_ffn_Bu"])
    for e in range(NE):
        rr = (e % 3) * 32
        cc = (e // 3) * DFF
        bgts[rr:rr + 16, cc:cc + DFF] = Bg[e].T
        buts[rr:rr + 16, cc:cc + DFF] = Bu[e].T
    h["bgts"] = bgts
    h["buts"] = buts
    h["sel16r"] = np.repeat(np.eye(8, dtype=np.float32), 16, axis=1)
    h["tri"] = np.tril(np.ones((128, 128), dtype=np.float32)).T.copy()
    h["identity"] = np.eye(128, dtype=np.float32)
    return h


SCRATCH = {
    "xn0b": BF16, "xn0f": F32, "out": F32,
    "s0sc": F32, "g0sc": F32, "x1sc": F32, "xn1f": F32, "xn1b": BF16,
    "xinsc": F32, "s1sc": F32, "base1": F32,
}

IN_SPECS = {
    "xT": ([D, T], F32),
    "wqkv": ([D, 3 * D], F32), "bqkv": ([128, 48], F32),
    "wout": ([D, D], F32), "bout": ([128, 16], F32),
    "gatew": ([D, 8], F32),
    "fc1": ([NE, D, D], F32), "fc1b": ([NE, 128, 16], F32),
    "fc2": ([NE, D, D], F32), "fc2b": ([NE, D], F32),
    "alpha": ([1, 8], F32),
    "ln1w": ([128, 16], F32), "ln1b": ([128, 16], F32),
    "ln2w": ([128, 16], F32), "ln2b": ([128, 16], F32),
    "tmu": ([128, 16], F32), "tmw": ([128, 16], F32),
    "tmmixk": ([128, 16], F32), "tmmixv": ([128, 16], F32),
    "tmmixr": ([128, 16], F32), "tmbo": ([128, 16], F32),
    "cmmixk": ([128, 16], F32), "cmmixr": ([128, 16], F32),
    "tmwk": ([D, D], F32), "tmwv": ([D, D], F32), "tmwr": ([D, D], F32),
    "tmwo": ([D, D], F32),
    "cmwk": ([D, D], F32), "cmwv": ([D, D], F32), "cmwr": ([D, D], F32),
    "wgT": ([D, DFF], F32), "wuT": ([D, DFF], F32), "wdT": ([DFF, D], F32),
    "wr8T": ([D, 8], F32),
    "agT": ([D, 128], F32), "auT": ([D, 128], F32),
    "bgT": ([R, NE, DFF], F32), "buT": ([R, NE, DFF], F32),
    "adT": ([DFF, NE, R], F32), "bdT": ([128, D], F32),
    "bgts": ([128, 3 * DFF], F32), "buts": ([128, 3 * DFF], F32),
    "sel16r": ([8, 128], F32),
    "tri": ([128, 128], F32), "identity": ([128, 128], F32),
}

PHASE_NEEDS = {
    "ln0": ["xT"],
    "attn0": ["wqkv", "bqkv", "wout", "bout", "tri"],
    "moe0": ["gatew", "fc1", "fc1b", "fc2", "fc2b", "alpha", "identity", "sel"],
    "epi0": ["xT"],
    "rwkv1": ["ln1w", "ln1b", "ln2w", "ln2b", "tmu", "tmw", "tmmixk", "tmmixv",
              "tmmixr", "tmbo", "cmmixk", "cmmixr", "tmwk", "tmwv", "tmwr",
              "tmwo", "cmwk", "cmwv", "cmwr"],
    "ffn1": ["wgT", "wuT", "wdT", "wr8T", "agT", "auT", "bgts", "buts", "adT",
             "bdT", "identity", "sel16r"],
    "epi1": [],
}


class Dev:

    def __init__(self, nc, p, needs, sin, sout):
        self.nc, self.p = nc, p
        self.tn = {}
        for name in needs:
            shape, dt = IN_SPECS[name]
            self.tn[name] = nc.dram_tensor(name, shape, dt, kind="ExternalInput").ap()
        for name in sin:
            self.tn[name] = nc.dram_tensor(name, [D, T], SCRATCH[name],
                                           kind="ExternalInput").ap()
        for name in sout:
            kind = "ExternalOutput" if name == "out" else sout[name]
            dt = F32 if name == "out" else SCRATCH[name]
            self.tn[name] = nc.dram_tensor(name, [D, T], dt, kind=kind).ap()
        self.R = nc.alloc_sbuf_tensor("arR", [128, NB, T], BF16).ap()
        self.W = nc.alloc_sbuf_tensor("arW", [128, NB, T], BF16).ap()
        self.F = nc.alloc_sbuf_tensor("arF", [128, 7, T], F32).ap()
        self.PS = nc.alloc_psum_tensor("psA", [128, 4096], F32).ap()
        self.ones1x128 = nc.alloc_sbuf_tensor("ones1x128", [1, 128], F32).ap()
        self.onesb = nc.alloc_sbuf_tensor("onesb", [128, 1], BF16).ap()
        self.onesf = nc.alloc_sbuf_tensor("onesf", [128, 1], F32).ap()
        self.ccol = {}
        self._ccol_t = nc.alloc_sbuf_tensor("ccols", [128, 16], F32).ap()
        self._ccol_n = 0
        self.smalls = {}

    def fslot(self, i):
        return self.F[:, i, :]

    def fslot_bf(self, i):
        return self.F[:, i, :].bitcast(BF16)

    def bank(self, b, n=512):
        return self.PS[:, b * 512: b * 512 + n]

    def ccol_init(self, vals):
        p = self.p
        toks = []
        for v in vals:
            if v in self.ccol:
                continue
            i = self._ccol_n
            self._ccol_n += 1
            ap = self._ccol_t[:, i:i + 1]
            self.ccol[v] = ap
            toks.append(p.dv(lambda e, ap=ap, v=v: e.memset(ap, v), inc=True))
        t1 = p.dv(lambda e: e.memset(self.ones1x128, 1.0), inc=True)
        t2 = p.dv(lambda e: e.memset(self.onesb, 1.0), inc=True)
        t3 = p.dv(lambda e: e.memset(self.onesf, 1.0), inc=True)
        self.const_tok = t3
        return t3


def ln_phase(C, src, dst_sbuf=None, affine=None, out_f32=None, out_bf16=None,
             fp32_stats=False, skip_stats=False, stats_loc=None):
    p, PS = C.p, C.PS
    xb = C.fslot_bf(0)
    xb2 = C.fslot_bf(1)
    cons = [[], []]
    bufcons = [[], []]
    petok = None
    for db in ([] if skip_stats else range(NB)):
        sl = db % 2
        fx = C.fslot(2 + sl)
        ld = p.dma("sync", fx, src[db * 128:(db + 1) * 128, :], waits=cons[sl])
        if fp32_stats:
            fsq = C.fslot(4 + sl)
            tsq = p.ac(lambda e, f=fx, g=fsq: e.activation(g, f, AF.Square),
                       waits=[ld] + bufcons[sl], inc=True)
            cons[sl] = [tsq]
            for tc in range(4):
                p.pe(lambda e, f=fx, tc=tc, db=db: e.matmul(
                    PS[0:1, tc * 512:(tc + 1) * 512], C.onesf,
                    f[:, tc * 512:(tc + 1) * 512],
                    start=(db == 0), stop=(db == NB - 1)),
                    waits=[C.const_tok, ld, tsq] + bufcons[sl])
            for tc in range(4):
                petok = p.pe(lambda e, g=fsq, tc=tc, db=db: e.matmul(
                    PS[0:1, 2048 + tc * 512: 2048 + (tc + 1) * 512], C.onesf,
                    g[:, tc * 512:(tc + 1) * 512],
                    start=(db == 0), stop=(db == NB - 1)),
                    waits=[tsq], inc=(tc == 3))
            bufcons[sl] = [petok]
            cons[sl] = [petok]
            continue
        buf = (xb, xb2)[sl]
        tcast = p.dv(lambda e, b=buf, f=fx: e.tensor_copy(b[:, 0:2048], f),
                     waits=[ld] + bufcons[sl], inc=True)
        tsq = p.ac(lambda e, b=buf, f=fx: e.activation(b[:, 2048:4096], f, AF.Square),
                   waits=[ld] + bufcons[sl], inc=True)
        cons[sl] = [tcast, tsq]
        for tc in range(4):
            p.pe(lambda e, b=buf, tc=tc, db=db: e.matmul(
                PS[0:1, tc * 512:(tc + 1) * 512], C.onesb,
                b[:, tc * 512:(tc + 1) * 512],
                start=(db == 0), stop=(db == NB - 1)), waits=[tcast, tsq])
        for tc in range(4):
            petok = p.pe(lambda e, b=buf, tc=tc, db=db: e.matmul(
                PS[0:1, 2048 + tc * 512: 2048 + (tc + 1) * 512], C.onesb,
                b[:, 2048 + tc * 512: 2048 + (tc + 1) * 512],
                start=(db == 0), stop=(db == NB - 1)),
                waits=[tcast, tsq], inc=(tc == 3))
        bufcons[sl] = [petok]
    lnf = C.F.rearrange("p a b -> p (a b)")
    pS1 = stats_loc[0] if stats_loc else PS[0:1, 0:2048]
    pS2 = stats_loc[1] if stats_loc else PS[0:1, 2048:4096]
    rS1, rS2 = lnf[0:1, 0:2048], lnf[0:1, 2048:4096]
    rMean = lnf[0:1, 12288:14336]
    rVar = lnf[0:1, 0:2048]
    rRstd = lnf[0:1, 2048:4096]
    rMr = lnf[0:1, 0:2048]
    p.dv(lambda e: e.tensor_copy(rS1, pS1), waits=[petok])
    p.dv(lambda e: e.tensor_copy(rS2, pS2))
    p.dv(lambda e: e.tensor_scalar_mul(rMean, rS1, 1.0 / D))
    p.dv(lambda e: e.tensor_mul(rVar, rMean, rMean))
    tv = p.dv(lambda e: e.scalar_tensor_tensor(rVar, rS2, 1.0 / D, rVar,
                                               ALU.mult, ALU.subtract), inc=True)
    eps = C.ccol[1e-5][0:1, :]
    p.ac(lambda e: e.activation(rRstd, rVar, AF.Ln, bias=eps), waits=[tv])
    ta = p.ac(lambda e: e.activation(rRstd, rRstd, AF.Exp, scale=-0.5), inc=True)
    trows = p.dv(lambda e: e.tensor_mul(rMr, rMean, rRstd), waits=[ta], inc=True)
    bt = None
    for tc in range(4):
        p.pe(lambda e, tc=tc: e.matmul(
            PS[:, tc * 512:(tc + 1) * 512], C.ones1x128,
            rRstd[:, tc * 512:(tc + 1) * 512], start=True, stop=True),
            waits=[trows])
        bt = p.pe(lambda e, tc=tc: e.matmul(
            PS[:, 2048 + tc * 512: 2048 + (tc + 1) * 512], C.ones1x128,
            rMr[:, tc * 512:(tc + 1) * 512], start=True, stop=True),
            waits=[trows], inc=(tc == 3))
    consx = [cons[0], cons[1]]
    const_ = [[], []]
    for db in range(NB):
        sl = db % 2
        fx, ftmp = C.fslot(2 + sl), C.fslot(4 + sl)
        ld = p.dma("sync", fx, src[db * 128:(db + 1) * 128, :], waits=consx[sl])
        t1 = p.dv(lambda e, f=fx, g=ftmp: e.tensor_mul(g, f, PS[:, 0:2048]),
                  waits=[ld, bt] + const_[sl], inc=True)
        consx[sl] = [t1]
        t2 = p.dv(lambda e, g=ftmp: e.tensor_sub(g, g, PS[:, 2048:4096]), inc=True)
        tcons = [t2]
        if out_f32 is not None:
            tcons.append(p.dma("sync", out_f32[db * 128:(db + 1) * 128, :], ftmp,
                               waits=[t2]))
        dst = dst_sbuf(db) if dst_sbuf else None
        if affine:
            wv, bv = affine
            t3 = p.dv(lambda e, g=ftmp, db=db, dst=dst: e.tensor_scalar(
                dst, g, wv[:, db:db + 1], bv[:, db:db + 1], ALU.mult, ALU.add),
                inc=True)
            tcons.append(t3)
        elif dst is not None:
            t3 = p.dv(lambda e, g=ftmp, dst=dst: e.tensor_copy(dst, g), inc=True)
            tcons.append(t3)
        else:
            t3 = t2
        if out_bf16 is not None and dst is not None:
            tcons.append(p.dma("sync", out_bf16[db * 128:(db + 1) * 128, :], dst,
                               waits=[t3]))
        const_[sl] = tcons


SCALE_QK = 1.0 / float(np.sqrt(DH))


def attn0_phase(C, xn_blocks, s_out_hbm):
    import os
    STAGE = int(os.environ.get("ATTN_STAGE", "5"))
    p, PS = C.p, C.PS
    Ff = C.F.rearrange("p a b -> p (a b)")
    FB = Ff.bitcast(BF16)
    SW = [FB[:, 0:6144].rearrange("p (a b) -> p a b", b=384),
          FB[:, 6144:12288].rearrange("p (a b) -> p a b", b=384)]
    QH = FB[:, 12288:14336]
    KH = FB[:, 14336:16384]
    VH = FB[:, 16384:18432].rearrange("p (a b) -> p a b", b=128)
    PT = [FB[:, 18432:18944], FB[:, 18944:19456]]
    TRI = FB[:, 19456:19584]
    SWO = [FB[:, 20480:22528].rearrange("p (a b) -> p a b", b=128),
           FB[:, 22528:24576].rearrange("p (a b) -> p a b", b=128)]
    SOUT = Ff[:, 12288:14336]
    NRMR = [Ff[0:1, 12288 + i * 512:12288 + (i + 1) * 512] for i in range(4)]
    RR = [Ff[0:1, 0:2048], Ff[0:1, 2048:4096]]
    wqkv = C.tn["wqkv"]
    scale_ap = C.ccol[SCALE_QK]

    ttri = p.dma("gpsimd", TRI, C.tn["tri"])

    sw_cons = [[], []]
    ld_toks = [None, None]

    def load_head(h):
        sl = h % 2
        toks = []
        for j, off in enumerate((h * 128, D + h * 128, 2 * D + h * 128)):
            toks.append(p.dma(
                "gpsimd", SW[sl][:, :, j * 128:(j + 1) * 128],
                wqkv[:, off:off + 128].rearrange("(c p) m -> p c m", p=128),
                waits=sw_cons[sl]))
        ld_toks[sl] = toks

    if STAGE < 1:
        return
    load_head(0)
    nrmd = C.tn["nrmdram"]
    nrm_st = [[], [], [], []]
    evq_tok = [None] * 8
    pv_evac_tok = None
    nrm_evac_tok = None
    sbank_user = [None, None]
    pt_user = [None, None]
    for h in range(NH):
        sl = h % 2
        if h + 1 < NH:
            load_head(h + 1)
        ldt = ld_toks[sl]
        if STAGE < 1.5 and False:
            continue
        qkv_mm_toks = []
        sbq = C.smalls["bqkv"]
        for j, dstbuf in ((0, QH), (1, KH)):
            bcol = sbq[:, 16 * j + h:16 * j + h + 1]
            for tc in range(4):
                bk = tc % 2
                for db in range(NB):
                    tok = p.pe(lambda e, sl=sl, j=j, db=db, tc=tc, bk=bk: e.matmul(
                        C.bank(bk)[:, 0:512], SW[sl][:, db, j * 128:(j + 1) * 128],
                        xn_blocks(db)[:, tc * 512:(tc + 1) * 512],
                        start=(db == 0), stop=(db == NB - 1)),
                        waits=(ldt + [evq_tok[bk]] + C.smalls_tok) if db == 0 else [],
                        inc=(db == NB - 1))
                evq_tok[bk] = p.ac(lambda e, d=dstbuf, tc=tc, bk=bk, bcol=bcol:
                    e.activation(d[:, tc * 512:(tc + 1) * 512],
                                 C.bank(bk)[:, 0:512], AF.Identity, bias=bcol),
                    waits=[tok], inc=True)
                qkv_mm_toks.append(tok)
        for tt in range(NB):
            bk = 2 + tt % 2
            for db in range(NB):
                tok = p.pe(lambda e, sl=sl, db=db, tt=tt, bk=bk: e.matmul(
                    C.bank(bk)[:, 0:128],
                    xn_blocks(db)[:, tt * 128:(tt + 1) * 128],
                    SW[sl][:, db, 256:384],
                    start=(db == 0), stop=(db == NB - 1)),
                    waits=([evq_tok[bk]] if db == 0 else []),
                    inc=(db == NB - 1))
            evq_tok[bk] = p.ac(lambda e, tt=tt, bk=bk: e.copy(
                VH[:, tt, :], C.bank(bk)[:, 0:128]), waits=[tok], inc=True)
            qkv_mm_toks.append(tok)
        sw_cons[sl] = qkv_mm_toks
        qkv_ev = [evq_tok[0], evq_tok[1], evq_tok[2], evq_tok[3]]
        if STAGE < 2:
            continue
        for qc in range(4):
            nkt = 4 * qc + 4
            s_toks = {}
            exp_toks = {}

            def emit_S(kt, qc=qc, s_toks=s_toks):
                par = kt % 2
                w = qkv_ev + [sbank_user[par]]
                s_toks[kt] = p.pe(lambda e, kt=kt, qc=qc: e.matmul(
                    C.bank(4 + kt % 2)[:, 0:512], KH[:, kt * 128:(kt + 1) * 128],
                    QH[:, qc * 512:(qc + 1) * 512], start=True, stop=True),
                    waits=w, inc=True)

            def emit_exp_mask(kt, qc=qc, s_toks=s_toks, exp_toks=exp_toks):
                par = kt % 2
                ptb = PT[par]
                te = p.ac(lambda e, kt=kt, ptb=ptb: e.activation(
                    ptb, C.bank(4 + kt % 2)[:, 0:512], AF.Exp, scale=scale_ap),
                    waits=[s_toks[kt], pt_user[par], C.const_tok], inc=True)
                sbank_user[par] = te
                if kt >= 4 * qc:
                    off = kt * 128 - qc * 512
                    if off > 0:
                        te = p.dv(lambda e, ptb=ptb, off=off: e.memset(
                            ptb[:, 0:off], 0.0), waits=[te], inc=True)
                    te = p.dv(lambda e, ptb=ptb, off=off: e.tensor_mul(
                        ptb[:, off:off + 128], ptb[:, off:off + 128], TRI),
                        waits=[te, ttri], inc=True)
                exp_toks[kt] = te

            def emit_PV(kt, qc=qc, nkt=nkt, exp_toks=exp_toks):
                nonlocal pv_evac_tok, nrm_evac_tok
                par = kt % 2
                ptb = PT[par]
                w = [exp_toks[kt]] + ([pv_evac_tok] if kt == 0 else [])
                p.pe(lambda e, kt=kt, ptb=ptb: e.matmul(
                    C.bank(6)[:, 0:512], VH[:, kt, :], ptb,
                    start=(kt == 0), stop=(kt == nkt - 1)), waits=w)
                w2 = [nrm_evac_tok] if kt == 0 else []
                pt_user[par] = p.pe(lambda e, kt=kt, ptb=ptb, nkt=nkt: e.matmul(
                    PS[0:1, 3584:4096], C.onesb, ptb,
                    start=(kt == 0), stop=(kt == nkt - 1)), waits=w2, inc=True)

            emit_S(0)
            for kt in range(nkt):
                emit_exp_mask(kt)
                if kt + 1 < nkt:
                    emit_S(kt + 1)
                if STAGE >= 3:
                    emit_PV(kt)
            if STAGE < 3:
                continue
            pvt = pt_user[(nkt - 1) % 2]
            pv_evac_tok = p.dv(lambda e, h=h, qc=qc: e.tensor_copy(
                C.W[:, h, qc * 512:(qc + 1) * 512], C.bank(6)[:, 0:512]),
                waits=[pvt], inc=True)
            r = (h * 4 + qc) % 4
            nrm_evac_tok = p.dv(lambda e, r=r: e.tensor_copy(
                NRMR[r], PS[0:1, 3584:4096]),
                waits=[pvt] + nrm_st[r], inc=True)
            nrm_st[r] = [p.dma(
                "sync", nrmd[0:1, (h * 4 + qc) * 512:(h * 4 + qc + 1) * 512],
                NRMR[r], waits=[nrm_evac_tok])]
    p.barrier()
    if STAGE < 4:
        return
    n16a = C.F[:, 0, :][0:16, :]
    n16b = C.F[:, 1, :][0:16, :]
    n16c = C.F[:, 2, :][0:16, :]
    t = p.dma("sync", n16a, nrmd.rearrange("o (a b) -> (o a) b", a=16))
    tr = p.dv(lambda e: e.reciprocal(n16b, n16a), waits=[t], inc=True)
    t = p.dma("sync", nrmd.rearrange("o (a b) -> (o a) b", a=16), n16b,
              waits=[tr])
    p.barrier()
    bc_tok = {}
    rr_cons = [[], []]
    for h in range(NH):
        rsl = h % 2
        lt = p.dma("sync", RR[rsl], nrmd[0:1, h * T:(h + 1) * T],
                   waits=rr_cons[rsl])
        rr_cons[rsl] = []
        for tc in range(4):
            bk = (h * 4 + tc) % 4
            tb = p.pe(lambda e, h=h, tc=tc, bk=bk, rsl=rsl: e.matmul(
                C.bank(bk)[:, 0:512], C.ones1x128,
                RR[rsl][:, tc * 512:(tc + 1) * 512],
                start=True, stop=True), waits=[bc_tok.get(bk), lt], inc=True)
            rr_cons[rsl].append(tb)
            bc_tok[bk] = p.dv(lambda e, h=h, tc=tc, bk=bk: e.tensor_mul(
                C.W[:, h, tc * 512:(tc + 1) * 512],
                C.W[:, h, tc * 512:(tc + 1) * 512], C.bank(bk)[:, 0:512]),
                waits=[tb], inc=True)
    p.barrier()
    if STAGE < 5:
        return
    wout = C.tn["wout"]
    sbout = C.smalls["bout"]
    swo_cons = [[], []]
    ld2 = [None, None]

    def load_wo(ot):
        sl = ot % 2
        ld2[sl] = [p.dma("gpsimd", SWO[sl],
                         wout[:, ot * 128:(ot + 1) * 128].rearrange(
                             "(c p) m -> p c m", p=128), waits=swo_cons[sl])]

    load_wo(0)
    st_tok = []
    ev_tok = {}
    beff_user = [None]
    for ot in range(NB):
        sl = ot % 2
        if ot + 1 < NB:
            load_wo(ot + 1)
        mm_toks = []
        for cb in range(NB):
            btk = p.pe(lambda e, sl=sl, cb=cb: e.matmul(
                C.bank(7)[:, 0:1], SWO[sl][:, cb, :],
                C.smalls["bqkvb"][:, 32 + cb:33 + cb],
                start=(cb == 0), stop=(cb == NB - 1)),
                waits=(ld2[sl] + [beff_user[0]]) if cb == 0 else [],
                inc=(cb == NB - 1))
        beff_user[0] = p.dv(lambda e: e.tensor_copy(
            C.smalls["beff"], C.bank(7)[:, 0:1]), waits=[btk], inc=True)
        beffc = beff_user[0]
        for tc in range(4):
            bk = tc % 2
            for cb in range(NB):
                tok = p.pe(lambda e, sl=sl, cb=cb, tc=tc, bk=bk: e.matmul(
                    C.bank(bk)[:, 0:512], SWO[sl][:, cb, :],
                    C.W[:, cb, tc * 512:(tc + 1) * 512],
                    start=(cb == 0), stop=(cb == NB - 1)),
                    waits=(ld2[sl] + [ev_tok.get(bk)]) if cb == 0 else [],
                    inc=(cb == NB - 1))
            mm_toks.append(tok)
            ev_tok[bk] = p.dv(lambda e, ot=ot, tc=tc, bk=bk: e.tensor_scalar(
                SOUT[:, tc * 512:(tc + 1) * 512], C.bank(bk)[:, 0:512],
                sbout[:, ot:ot + 1], C.smalls["beff"][:, 0:1],
                ALU.add, ALU.add),
                waits=[tok, beffc] + ([st_tok[-1]] if st_tok else []),
                strict=True, inc=True)
        swo_cons[sl] = mm_toks
        st_tok.append(p.dma("sync", s_out_hbm[ot * 128:(ot + 1) * 128, :], SOUT,
                            waits=[ev_tok[0], ev_tok[1]]))
    p.barrier()


def moe0_phase(C, xn_blocks, g_out_hbm):
    import os
    STAGE = int(os.environ.get("MOE_STAGE", "9"))
    p, PS = C.p, C.PS
    Ff = C.F.rearrange("p a b -> p (a b)")
    FB = Ff.bitcast(BF16)
    CB = Ff[:, 0:2048]
    STG = [Ff[:, 2048:4096], Ff[:, 4096:6144]]
    comb = Ff[:, 6656:6784].rearrange("p (a b) -> p a b", b=8)
    gcol = Ff[:, 6784:6800]
    l2s = Ff[:, 6800:6808]
    eqs = Ff[:, 6808:6816]
    W1B = [FB[:, 16384:18432].rearrange("p (a b) -> p a b", b=128),
           FB[:, 18432:20480].rearrange("p (a b) -> p a b", b=128)]
    W2B = [FB[:, 20480:22528].rearrange("p (a b) -> p a b", b=128),
           FB[:, 22528:24576].rearrange("p (a b) -> p a b", b=128)]
    SGW = FB[:, 26624:26752].rearrange("p (a b) -> p a b", b=8)
    comb8b = C.smalls["comb8b"]
    alpt = C.smalls["alphat"]
    sfc2b, sel, idn = C.smalls["fc2b"], C.smalls["sel"], C.smalls["idn"]
    fc1b = C.smalls["fc1b"]
    fc1h, fc2h = C.tn["fc1"], C.tn["fc2"]

    sgwf = C.smalls["gatewf"].rearrange("p (c m) -> p c m", m=8)
    xn0f = C.tn["xn0f"]
    ta = p.pe(lambda e: e.matmul(C.bank(7)[:, 0:8], C.ones1x128,
                                 C.smalls["alpha"], start=True, stop=True),
              waits=C.smalls_tok + [C.const_tok], inc=True)
    ta = p.dv(lambda e: e.tensor_copy(alpt, C.bank(7)[:, 0:8]), waits=[ta], inc=True)
    logacc = Ff[:, 6816:6944]
    gcons = [[], []]
    xfr = [STG[0], STG[1]]
    gacc = None
    for db in range(NB):
        sl = db % 2
        ld = p.dma("sync", xfr[sl], xn0f[db * 128:(db + 1) * 128, :],
                   waits=gcons[sl])
        for tt in range(NB):
            tok = p.pe(lambda e, tt=tt, db=db, sl=sl: e.matmul(
                C.bank(0)[:, tt * 8:(tt + 1) * 8],
                xfr[sl][:, tt * 128:(tt + 1) * 128], sgwf[:, db, :],
                start=True, stop=True),
                waits=[ld, gacc], inc=(tt == NB - 1))
        if db == 0:
            gacc = p.dv(lambda e: e.tensor_copy(logacc, C.bank(0)[:, 0:128]),
                        waits=[tok], inc=True)
        else:
            gacc = p.dv(lambda e: e.tensor_add(logacc, logacc,
                                               C.bank(0)[:, 0:128]),
                        waits=[tok, gacc], strict=True, inc=True)
        gcons[sl] = [tok]
    lg_toks = [gacc] * NB
    if STAGE < 2:
        return
    m1, m2, nm1, den, e2c = (gcol[:, 0:1], gcol[:, 1:2], gcol[:, 2:3],
                             gcol[:, 3:4], gcol[:, 4:5])
    if os.environ.get("MOE_DBG"):
        dbgl = C.nc.dram_tensor("dbgl", [128, 128], F32, kind="ExternalOutput").ap()
        tdl = p.dv(lambda e: e.tensor_copy(Ff[:, 6816:6944], C.bank(0)[:, 0:128]),
                   waits=[lg_toks[NB - 1]], inc=True)
        p.dma("sync", dbgl, Ff[:, 6816:6944], waits=[tdl])
    ct_toks = []
    for tt in range(NB):
        L = logacc[:, tt * 8:(tt + 1) * 8]
        t0 = p.dv(lambda e, L=L: e.reduce_max(m1, L, axis=AX.X),
                  waits=[lg_toks[NB - 1], ta], inc=True)
        t1 = p.dv(lambda e, L=L: e.tensor_single_scalar(eqs, L, m1, ALU.is_ge),
                  waits=[t0], strict=True, inc=True)
        t2 = p.dv(lambda e, L=L: e.scalar_tensor_tensor(l2s, eqs, -1e30, L,
                                                        ALU.mult, ALU.add),
                  waits=[t1], strict=True, inc=True)
        t3 = p.dv(lambda e: e.reduce_max(m2, l2s, axis=AX.X),
                  waits=[t2], strict=True, inc=True)
        t4 = p.dv(lambda e: e.tensor_scalar_mul(nm1, m1, -1.0),
                  waits=[t3], strict=True, inc=True)
        ta2 = p.ac(lambda e, L=L: e.activation(l2s, L, AF.Exp, bias=nm1),
                   waits=[t4])
        ta2 = p.ac(lambda e: e.activation(e2c, m2, AF.Exp, bias=nm1), inc=True)
        t5 = p.dv(lambda e: e.tensor_scalar_add(e2c, e2c, 1.0), waits=[ta2],
                  inc=True)
        t6 = p.dv(lambda e: e.reciprocal(den, e2c), waits=[t5], strict=True,
                  inc=True)
        t7 = p.dv(lambda e, L=L: e.tensor_single_scalar(eqs, L, m2, ALU.is_ge),
                  waits=[t6], strict=True, inc=True)
        t8 = p.dv(lambda e: e.tensor_mul(l2s, l2s, eqs), waits=[t7],
                  strict=True, inc=True)
        t9 = p.dv(lambda e: e.tensor_scalar_mul(l2s, l2s, den), waits=[t8],
                  strict=True, inc=True)
        t = p.dv(lambda e, tt=tt: e.tensor_mul(comb[:, tt, :], l2s, alpt),
                 waits=[t9], strict=True, inc=True)
        ct_toks.append(t)
        if tt == 0 and os.environ.get("MOE_DBG2"):
            dbgg = C.nc.dram_tensor("dbgg", [128, 48], F32,
                                    kind="ExternalOutput").ap()
            dt1 = p.dma("sync", dbgg[:, 0:16], gcol, waits=[t])
            dt2 = p.dma("sync", dbgg[:, 16:24], l2s, waits=[t])
            dt3 = p.dma("sync", dbgg[:, 24:32], eqs, waits=[t])
            dt4 = p.dma("sync", dbgg[:, 32:40], comb[:, 0, :], waits=[t])
        if os.environ.get("MOE_DBG2"):
            return
    prev_tr = []
    for tt in range(NB):
        t = p.pe(lambda e, tt=tt: e.transpose(
            PS[0:8, 3584:3712], comb[:, tt, :], idn),
            waits=[ct_toks[tt]] + prev_tr, inc=True)
        t2 = p.dv(lambda e, tt=tt: e.tensor_copy(
            comb8b[0:8, tt * 128:(tt + 1) * 128], PS[0:8, 3584:3712]),
            waits=[t], inc=True)
        prev_tr = [t2]
    tcb = prev_tr[-1]
    if os.environ.get("MOE_DBG"):
        dbg = C.nc.dram_tensor("dbg8", [8, T], F32, kind="ExternalOutput").ap()
        p.dma("sync", dbg, comb8, waits=[tcb])
    p.barrier()
    if STAGE < 4:
        return
    w1t = [None, None]
    w2t = [None, None]
    w1cons = [[], []]
    w2cons = [[], []]

    def load_w1(e_, ib):
        sl = ib % 2
        w1t[sl] = p.dma("gpsimd", W1B[sl],
                        fc1h[e_][:, ib * 128:(ib + 1) * 128].rearrange(
                            "(c p) m -> p c m", p=128), waits=w1cons[sl])

    def load_w2(e_, ht):
        sl = ht % 2
        w2t[sl] = p.dma("gpsimd", W2B[sl],
                        fc2h[e_][:, ht * 128:(ht + 1) * 128].rearrange(
                            "(c p) m -> p c m", p=128), waits=w2cons[sl])

    gel_toks = [None] * NB
    f1_user = [None, None, None]
    f2_user = [None] * 4
    stg_cons = [[], []]
    cb_tok = []
    acc_tok = None
    load_w1(0, 0)
    for e_ in range(NE):
        prev = cb_tok
        cb_tok = []
        for tc in range(4):
            t = p.pe(lambda e, e_=e_, tc=tc: e.matmul(
                C.bank(3)[:, 0:512], sel[0:8, e_ * 128:(e_ + 1) * 128],
                comb8b[0:8, tc * 512:(tc + 1) * 512], start=True, stop=True),
                waits=[tcb] + prev + cb_tok, inc=True)
            cb_tok.append(p.dv(lambda e, tc=tc: e.tensor_copy(
                CB[:, tc * 512:(tc + 1) * 512], C.bank(3)[:, 0:512]),
                waits=[t], inc=True))
        for ib in range(NB):
            sl = ib % 2
            mt = []
            for tc in range(4):
                bk = (ib * 4 + tc) % 3
                for db in range(NB):
                    tok = p.pe(lambda e, sl=sl, db=db, tc=tc, bk=bk: e.matmul(
                        C.bank(bk)[:, 0:512], W1B[sl][:, db, :],
                        xn_blocks(db)[:, tc * 512:(tc + 1) * 512],
                        start=(db == 0), stop=(db == NB - 1)),
                        waits=([w1t[sl], f1_user[bk]] if db == 0 else []),
                        inc=(db == NB - 1))
                f1_user[bk] = p.ac(lambda e, e_=e_, ib=ib, tc=tc, bk=bk: e.activation(
                    C.W[:, ib, tc * 512:(tc + 1) * 512], C.bank(bk)[:, 0:512],
                    AF.Gelu, bias=fc1b[:, e_ * 16 + ib:e_ * 16 + ib + 1]),
                    waits=[tok], inc=True)
                mt.append(tok)
            gel_toks[ib] = f1_user[(ib * 4 + 3) % 3]
            w1cons[sl] = mt
            if ib + 1 < NB:
                load_w1(e_, ib + 1)
        if STAGE < 5:
            continue
        load_w2(e_, 0)
        hw = [gel_toks[NB - 1]]
        for ht in range(NB):
            sl = ht % 2
            stg = STG[sl]
            mt = []
            lastmm = {}
            for ib in range(NB):
                for tc in range(4):
                    w = []
                    if ib == 0:
                        w = [w2t[sl], f2_user[tc]] + hw
                    stop = (ib == NB - 1) and (e_ != 0)
                    lastmm[tc] = p.pe(lambda e, sl=sl, ib=ib, tc=tc, stop=stop:
                                      e.matmul(
                        C.bank(4 + tc)[:, 0:512], W2B[sl][:, ib, :],
                        C.W[:, ib, tc * 512:(tc + 1) * 512],
                        start=(ib == 0), stop=stop),
                        waits=w, inc=stop)
                    mt.append(lastmm[tc])
            if e_ == 0:
                for tc in range(4):
                    lastmm[tc] = p.pe(lambda e, tc=tc, ht=ht: e.matmul(
                        C.bank(4 + tc)[:, 0:512],
                        sfc2b[0:8, ht * 128:(ht + 1) * 128],
                        comb8b[0:8, tc * 512:(tc + 1) * 512],
                        start=False, stop=True), inc=True)
            w2cons[sl] = mt
            if ht + 1 < NB:
                load_w2(e_, ht + 1)
            elif e_ + 1 < NE:
                load_w1(e_ + 1, 0)
            evs = []
            for tc in range(4):
                f2_user[tc] = p.dv(lambda e, tc=tc, stg=stg: e.tensor_mul(
                    stg[:, tc * 512:(tc + 1) * 512], C.bank(4 + tc)[:, 0:512],
                    CB[:, tc * 512:(tc + 1) * 512]),
                    waits=[lastmm[tc], cb_tok[tc]] + stg_cons[sl], inc=True)
                evs.append(f2_user[tc])
            stg_cons[sl] = [p.dma(
                "gpsimd", g_out_hbm[ht * 128:(ht + 1) * 128, :], stg,
                waits=evs, accum_op=(ALU.bypass if e_ == 0 else ALU.add))]
    p.barrier()


def epi_phase(C, x_src, s_src, g_src, dst_hbm, stats=False):
    p, PS = C.p, C.PS
    cons = [[], [], [], [], [], []]
    sqcons = []
    fsq = C.fslot(6)
    petok = None
    for db in range(NB):
        sl = db % 2
        fx, fs, fg = C.fslot(0 + sl), C.fslot(2 + sl), C.fslot(4 + sl)
        r = db * 128
        lx = p.dma("sync", fx, x_src[r:r + 128, :], waits=cons[0 + sl])
        ls = p.dma("sync", fs, s_src[r:r + 128, :], waits=cons[2 + sl])
        lg = p.dma("sync", fg, g_src[r:r + 128, :], waits=cons[4 + sl])
        t1 = p.ac(lambda e, fg=fg: e.activation(fg, fg, AF.Silu),
                  waits=[lg], inc=True)
        t2 = p.dv(lambda e, fs=fs, fg=fg: e.tensor_mul(fs, fs, fg),
                  waits=[ls, t1], inc=True)
        t3 = p.dv(lambda e, fx=fx, fs=fs: e.tensor_add(fx, fx, fs),
                  waits=[lx], inc=True)
        st = p.dma("sync", dst_hbm[r:r + 128, :], fx, waits=[t3])
        cons[2 + sl] = [t3]
        cons[4 + sl] = [t2]
        if stats:
            tsq = p.ac(lambda e, fx=fx: e.activation(fsq, fx, AF.Square),
                       waits=[t3] + sqcons, inc=True)
            for tc in range(4):
                p.pe(lambda e, fx=fx, tc=tc, db=db: e.matmul(
                    PS[0:1, tc * 512:(tc + 1) * 512], C.onesf,
                    fx[:, tc * 512:(tc + 1) * 512],
                    start=(db == 0), stop=(db == NB - 1)),
                    waits=[C.const_tok, t3])
            for tc in range(4):
                petok = p.pe(lambda e, tc=tc, db=db: e.matmul(
                    PS[0:1, 2048 + tc * 512: 2048 + (tc + 1) * 512], C.onesf,
                    fsq[:, tc * 512:(tc + 1) * 512],
                    start=(db == 0), stop=(db == NB - 1)),
                    waits=[tsq], inc=(tc == 3))
            sqcons = [petok]
            cons[0 + sl] = [st, petok]
        else:
            cons[0 + sl] = [st]
    p.barrier()


def loadR_phase(C, src_hbm_bf16):
    t = C.p.dma("sync", C.R, src_hbm_bf16.rearrange("(c p) t -> p c t", p=128))
    C.p.barrier()
    return t


def rwkv1_phase(C, s_out_hbm):
    p, PS = C.p, C.PS
    Ff = C.F.rearrange("p a b -> p (a b)")
    FB = Ff.bitcast(BF16)
    TC2 = 1024
    sm = C.smalls
    xn1f = C.tn["xn1f"]

    import os
    RDBG = os.environ.get("RWKV_DBG", "")
    ln_phase(C, xn1f, dst_sbuf=lambda db: C.R[:, db, :],
             affine=(sm["ln1w"], sm["ln1b"]))
    p.barrier()
    if RDBG:
        dbgr = C.nc.dram_tensor("dbgr", [D, T], BF16, kind="ExternalOutput").ap()
    if RDBG == "z":
        p.dma("sync", dbgr.rearrange("(c p) t -> p c t", p=128), C.R)
        p.barrier()
        return
    t0 = p.dv(lambda e: e.tensor_scalar_mul(sm["negw"], sm["tmw"], -1.0),
              inc=True)
    tsh = p.dv(lambda e: e.tensor_copy(sm["shiftz"], C.R[:, :, 1023:1024]),
               inc=True)
    p.barrier()

    KIN = C.W[:, :, 0:1024]
    VIN = C.W[:, :, 1024:2048]
    RIN = FB[:, 0:16384].rearrange("p (a b) -> p a b", b=1024)
    WRG = [FB[:, 16384 + i * 2048:16384 + (i + 1) * 2048].rearrange(
        "p (a b) -> p a b", b=128) for i in range(2)]
    S3 = Ff[:, 10240:11264]
    S0 = Ff[:, 11264:12288]
    S1 = Ff[:, 12288:13312]
    S2 = Ff[:, 13312:14336]
    DIF = FB[:, 24576:26624]
    CTMP = Ff[:, 13312:13313]

    def mixes(c, zsrc, mk_list, outs, strict_first):
        toks = []
        d_m = DIF[:, 0:TC2 - 1]
        ctmp = CTMP
        for db in range(NB):
            zc = zsrc(db)[:, c * TC2:(c + 1) * TC2]
            zs_main = zsrc(db)[:, c * TC2 - 1 + 1:(c + 1) * TC2 - 1] if c == 0 \
                else zsrc(db)[:, c * TC2:(c + 1) * TC2 - 1]
            zs_main = zsrc(db)[:, c * TC2:(c + 1) * TC2 - 1]
            td = p.dv(lambda e, zc=zc, zs=zs_main, d_m=d_m: e.tensor_sub(
                d_m, zc[:, 1:], zs), waits=strict_first, inc=True)
            tm2a = None
            if c == 1:
                tm2a = p.dv(lambda e, zc=zc, db=db: e.tensor_sub(
                    ctmp, zc[:, 0:1], sm["shiftz"][:, db:db + 1]), inc=True)
            for (mk, out, sig) in mk_list:
                o = outs(out, db)
                tm1 = p.dv(lambda e, d_m=d_m, zs=zs_main, mk=mk, o=o, db=db:
                           e.scalar_tensor_tensor(
                    o[:, 1:], d_m, mk[:, db:db + 1], zs, ALU.mult, ALU.add),
                    waits=[td], strict=True, inc=True)
                if c == 0:
                    tm2 = p.dv(lambda e, zc=zc, mk=mk, o=o, db=db:
                               e.tensor_single_scalar(
                        o[:, 0:1], zc[:, 0:1], mk[:, db:db + 1], ALU.mult),
                        inc=True)
                else:
                    tm2 = p.dv(lambda e, mk=mk, o=o, db=db:
                               e.scalar_tensor_tensor(
                        o[:, 0:1], ctmp, mk[:, db:db + 1],
                        sm["shiftz"][:, db:db + 1], ALU.mult, ALU.add),
                        waits=[tm2a], strict=True, inc=True)
                tk = [tm1, tm2]
                if sig:
                    tk = [p.ac(lambda e, o=o: e.activation(o, o, AF.Sigmoid),
                               waits=tk, inc=True)]
                toks.extend(tk)
        return toks

    wring_cons = [[], []]
    wr_tok = {}

    def load_w(name, ob, j):
        wh = C.tn[name]
        ridx = load_w.ctr % 2
        load_w.ctr += 1
        wr_tok[j] = (ridx, p.dma("gpsimd", WRG[ridx],
                                 wh[:, ob * 128:(ob + 1) * 128].rearrange(
                                     "(c p) m -> p c m", p=128),
                                 waits=wring_cons[ridx]))
    load_w.ctr = 0

    pb_user = [None] * 8
    carry_saved = []
    for c in range(2):
        mix_toks = mixes(
            c, lambda db: C.R[:, db, :],
            [(sm["tmmixk"], "k", False), (sm["tmmixv"], "v", False),
             (sm["tmmixr"], "r", True)],
            lambda out, db: {"k": KIN[:, db, :], "v": VIN[:, db, :],
                             "r": RIN[:, db, :]}[out],
            [tsh, t0] if c == 0 else [])
        mix_toks = mix_toks[-1:] if False else mix_toks
        for ob in range(NB):
            rb = 4 + 2 * (ob % 2)
            mmt = {}
            for j, buf, b0 in ((0, KIN, 0), (1, VIN, 2), (2, RIN, rb)):
                load_w(("tmwk", "tmwv", "tmwr")[j], ob, j)
                ridx, ldt = wr_tok[j]
                for half in range(2):
                    for db in range(NB):
                        w = []
                        if db == 0:
                            w = [ldt, pb_user[b0 + half]] + (
                                mix_toks if ob == 0 else [])
                        tok = p.pe(lambda e, ridx=ridx, buf=buf, half=half,
                                   db=db, b0=b0, c=c: e.matmul(
                            C.bank(b0 + half)[:, 0:512],
                            WRG[ridx][:, db, :],
                            buf[:, db, half * 512:(half + 1) * 512],
                            start=(db == 0), stop=(db == NB - 1)),
                            waits=w, inc=(db == NB - 1))
                    mmt[(j, half)] = tok
                wring_cons[ridx] = [mmt[(j, 0)], mmt[(j, 1)]]
            kps = PS[:, 0:1024]
            vps = PS[:, 1024:2048]
            rps = PS[:, rb * 512:(rb + 2) * 512]
            t1 = p.dv(lambda e, ob=ob: e.tensor_scalar(
                S0, kps, sm["tmu"][:, ob:ob + 1], None, ALU.add),
                waits=[mmt[(0, 0)], mmt[(0, 1)]], inc=True)
            pb_user[0] = pb_user[1] = t1
            t2 = p.dv(lambda e, ob=ob: e.tensor_scalar(
                S2, S0, sm["tmw"][:, ob:ob + 1], 0.0, ALU.subtract, ALU.min),
                waits=[t1], strict=True, inc=True)
            t2b = p.dv(lambda e, ob=ob: e.tensor_single_scalar(
                S0, S0, sm["tmw"][:, ob:ob + 1], ALU.max),
                waits=[t2], strict=True, inc=True)
            te1 = p.ac(lambda e, ob=ob: e.activation(
                S1, S0, AF.Exp, scale=-1.0, bias=sm["negw"][:, ob:ob + 1]),
                waits=[t2b], inc=True)
            te2 = p.ac(lambda e: e.activation(S2, S2, AF.Exp),
                       waits=[t2], inc=True)
            tb = p.dv(lambda e, c=c, ob=ob: e.tensor_tensor_scan(
                S0, S1, S2, 0.0 if c == 0 else sm["carryb"][:, ob:ob + 1],
                ALU.mult, ALU.add),
                waits=[te1, te2], strict=True, inc=True)
            save_b = None
            if c == 0:
                save_b = p.dv(lambda e, ob=ob: e.tensor_copy(
                    sm["carryb"][:, ob:ob + 1], S0[:, 1023:1024]),
                    waits=[tb], strict=True, inc=True)
            t3 = p.dv(lambda e: e.tensor_mul(S2, S2, vps),
                      waits=[mmt[(1, 0)], mmt[(1, 1)], tb], strict=True,
                      inc=True)
            pb_user[2] = pb_user[3] = t3
            ta_ = p.dv(lambda e, c=c, ob=ob: e.tensor_tensor_scan(
                S3, S1, S2, 0.0 if c == 0 else sm["carrya"][:, ob:ob + 1],
                ALU.mult, ALU.add),
                waits=[t3], strict=True, inc=True)
            save_a = None
            if c == 0:
                save_a = p.dv(lambda e, ob=ob: e.tensor_copy(
                    sm["carrya"][:, ob:ob + 1], S3[:, 1023:1024]),
                    waits=[ta_], strict=True, inc=True)
            tln = p.ac(lambda e: e.activation(S0, S0, AF.Ln),
                       waits=[tb, save_b], inc=True)
            tex = p.ac(lambda e: e.activation(S0, S0, AF.Exp, scale=-1.0),
                       inc=True)
            tw = p.dv(lambda e: e.tensor_mul(S2, S3, S0),
                      waits=[ta_, tex, save_a], strict=True, inc=True)
            pb_user[2] = pb_user[3] = tw
            twr = p.dv(lambda e, c=c, ob=ob, rps=rps: e.tensor_mul(
                C.R[:, ob, c * TC2:(c + 1) * TC2], S2, rps),
                waits=[tw, mmt[(2, 0)], mmt[(2, 1)]], strict=True, inc=True)
            pb_user[rb] = pb_user[rb + 1] = twr
    p.barrier()
    if RDBG == "wr":
        p.dma("sync", dbgr.rearrange("(c p) t -> p c t", p=128), C.R)
        p.barrier()
        return

    xinsc = C.tn["xinsc"]
    WOR = [FB[:, 0:2048].rearrange("p (a b) -> p a b", b=128),
           FB[:, 2048:4096].rearrange("p (a b) -> p a b", b=128)]
    XT = [Ff[:, 2048:4096], Ff[:, 4096:6144]]
    AT = [Ff[:, 6144:8192], Ff[:, 8192:10240]]
    wo_cons = [[], []]
    wo_tok = [None, None]

    def load_wo2(ot):
        slw = ot % 2
        wo_tok[slw] = p.dma("gpsimd", WOR[slw],
                            C.tn["tmwo"][:, ot * 128:(ot + 1) * 128].rearrange(
                                "(c p) m -> p c m", p=128), waits=wo_cons[slw])

    load_wo2(0)
    xld = [None, None]
    atok = {}
    st_cons = [[], []]
    SQC = Ff[:, 10240:12288]
    S1AC = Ff[0:1, 12288:14336]
    S2AC = Ff[32:33, 12288:14336]
    sqcons = []
    dvacc = None
    petok = None
    for ot in range(NB):
        slw = ot % 2
        if ot + 1 < NB:
            load_wo2(ot + 1)
        xld[slw] = p.dma("sync", XT[slw], xn1f[ot * 128:(ot + 1) * 128, :],
                         waits=st_cons[slw])
        mmts = []
        for tc in range(4):
            bk = tc % 2
            for os_ in range(NB):
                w = []
                if os_ == 0:
                    w = [wo_tok[slw], atok.get(bk)]
                tok = p.pe(lambda e, slw=slw, os_=os_, tc=tc, bk=bk: e.matmul(
                    C.bank(bk)[:, 0:512], WOR[slw][:, os_, :],
                    C.R[:, os_, tc * 512:(tc + 1) * 512],
                    start=(os_ == 0), stop=(os_ == NB - 1)),
                    waits=w, inc=(os_ == NB - 1))
            mmts.append(tok)
            t1 = p.dv(lambda e, slw=slw, tc=tc, bk=bk, ot=ot: e.tensor_scalar(
                AT[slw][:, tc * 512:(tc + 1) * 512], C.bank(bk)[:, 0:512],
                sm["tmbo"][:, ot:ot + 1], None, ALU.add),
                waits=[tok] + st_cons[slw], inc=True)
            atok[bk] = p.dv(lambda e, slw=slw, tc=tc: e.tensor_add(
                AT[slw][:, tc * 512:(tc + 1) * 512],
                AT[slw][:, tc * 512:(tc + 1) * 512],
                XT[slw][:, tc * 512:(tc + 1) * 512]),
                waits=[xld[slw], t1], strict=True, inc=True)
        wo_cons[slw] = mmts
        st_cons[slw] = [p.dma("sync", xinsc[ot * 128:(ot + 1) * 128, :],
                              AT[slw], waits=[atok[0], atok[1]])]
    p.barrier()

    ln_phase(C, xinsc, dst_sbuf=lambda db: C.R[:, db, :],
             affine=(sm["ln2w"], sm["ln2b"]))
    p.barrier()
    tsh2 = p.dv(lambda e: e.tensor_copy(sm["shiftz"], C.R[:, :, 1023:1024]),
                inc=True)
    p.barrier()
    if RDBG == "z2":
        p.dma("sync", dbgr.rearrange("(c p) t -> p c t", p=128), C.R)
        p.barrier()
        return

    if RDBG == "xin":
        return
    KIN2 = FB[:, 0:16384].rearrange("p (a b) -> p a b", b=1024)
    WRG2 = [FB[:, 16384 + i * 2048:16384 + (i + 1) * 2048].rearrange(
        "p (a b) -> p a b", b=128) for i in range(2)]
    T1 = Ff[:, 11264:12288]
    T2 = Ff[:, 12288:13312]
    ck_cons = [[], []]
    ck_tok = [None, None]

    def load_cmwk(ob):
        slw = ob % 2
        ck_tok[slw] = p.dma("gpsimd", WRG2[slw],
                            C.tn["cmwk"][:, ob * 128:(ob + 1) * 128].rearrange(
                                "(c p) m -> p c m", p=128), waits=ck_cons[slw])

    for c in range(2):
        mix_toks = mixes(
            c, lambda db: C.R[:, db, :],
            [(sm["cmmixk"], "k", False), (sm["cmmixr"], "r", False)],
            lambda out, db: {"k": KIN2[:, db, :],
                             "r": C.W[:, db, c * TC2:(c + 1) * TC2]}[out],
            [tsh2] if c == 0 else [])
        load_cmwk(0)
        for ob in range(NB):
            slw = ob % 2
            if ob + 1 < NB:
                load_cmwk(ob + 1)
            mmts = []
            for half in range(2):
                bk = half
                for db in range(NB):
                    w = []
                    if db == 0:
                        w = [ck_tok[slw], pb_user[bk]] +                             (mix_toks if ob == 0 else [])
                    tok = p.pe(lambda e, slw=slw, half=half, db=db, bk=bk:
                               e.matmul(
                        C.bank(bk)[:, 0:512], WRG2[slw][:, db, :],
                        KIN2[:, db, half * 512:(half + 1) * 512],
                        start=(db == 0), stop=(db == NB - 1)),
                        waits=w, inc=(db == NB - 1))
                mmts.append(tok)
            ck_cons[slw] = mmts
            t1 = p.dv(lambda e: e.tensor_scalar_max(T1, PS[:, 0:1024], 0.0),
                      waits=mmts, inc=True)
            pb_user[0] = pb_user[1] = t1
            t2 = p.dv(lambda e, c=c, ob=ob: e.tensor_mul(
                C.R[:, ob, c * TC2:(c + 1) * TC2], T1, T1),
                waits=[t1], strict=True, inc=True)
    p.barrier()
    if RDBG == "k2":
        p.dma("sync", dbgr.rearrange("(c p) t -> p c t", p=128), C.R)
        p.barrier()
        return

    WVR = [FB[:, 0:2048].rearrange("p (a b) -> p a b", b=128),
           FB[:, 2048:4096].rearrange("p (a b) -> p a b", b=128)]
    WRR = [FB[:, 4096:6144].rearrange("p (a b) -> p a b", b=128),
           FB[:, 6144:8192].rearrange("p (a b) -> p a b", b=128)]
    XT2 = [Ff[:, 4096:6144], Ff[:, 6144:8192]]
    OT2 = [Ff[:, 8192:10240], Ff[:, 10240:12288]]
    R2B = FB[:, 24576:25600]
    wv_cons = [[], []]
    wv_tok = [None, None]
    wq_cons = [[], []]
    wq_tok = [None, None]

    def load_cmwv(ot):
        slw = ot % 2
        wv_tok[slw] = p.dma("gpsimd", WVR[slw],
                            C.tn["cmwv"][:, ot * 128:(ot + 1) * 128].rearrange(
                                "(c p) m -> p c m", p=128), waits=wv_cons[slw])

    def load_cmwr(ot):
        slw = ot % 2
        wq_tok[slw] = p.dma("gpsimd", WRR[slw],
                            C.tn["cmwr"][:, ot * 128:(ot + 1) * 128].rearrange(
                                "(c p) m -> p c m", p=128), waits=wq_cons[slw])

    load_cmwv(0)
    load_cmwr(0)
    xld = [None, None]
    st2_cons = [[], []]
    r2tok = None
    last_t1 = None
    ot2tok = {}
    for ot in range(NB):
        slw = ot % 2
        if ot + 1 < NB:
            load_cmwv(ot + 1)
            load_cmwr(ot + 1)
        xld[slw] = p.dma("sync", XT2[slw], xinsc[ot * 128:(ot + 1) * 128, :],
                         waits=st2_cons[slw])
        mmv = []
        mmr = []
        for tc in range(4):
            bk = tc % 2
            bk2 = 2 + tc % 2
            for os_ in range(NB):
                wv_ = [wv_tok[slw], ot2tok.get(bk)] if os_ == 0 else []
                tokv = p.pe(lambda e, slw=slw, os_=os_, tc=tc, bk=bk: e.matmul(
                    C.bank(bk)[:, 0:512], WVR[slw][:, os_, :],
                    C.R[:, os_, tc * 512:(tc + 1) * 512],
                    start=(os_ == 0), stop=(os_ == NB - 1)),
                    waits=wv_, inc=(os_ == NB - 1))
                wr_ = [wq_tok[slw], r2tok] if os_ == 0 else []
                tokr = p.pe(lambda e, slw=slw, os_=os_, tc=tc, bk2=bk2:
                            e.matmul(
                    C.bank(bk2)[:, 0:512], WRR[slw][:, os_, :],
                    C.W[:, os_, tc * 512:(tc + 1) * 512],
                    start=(os_ == 0), stop=(os_ == NB - 1)),
                    waits=wr_, inc=(os_ == NB - 1))
            mmv.append(tokv)
            mmr.append(tokr)
            r2tok = p.ac(lambda e, tc=tc, bk2=bk2: e.activation(
                R2B[:, 0:512], C.bank(bk2)[:, 0:512], AF.Sigmoid),
                waits=[tokr, last_t1], inc=True)
            t1 = p.dv(lambda e, slw=slw, tc=tc, bk=bk: e.tensor_mul(
                OT2[slw][:, tc * 512:(tc + 1) * 512], C.bank(bk)[:, 0:512],
                R2B[:, 0:512]),
                waits=[tokv, r2tok] + st2_cons[slw], strict=True, inc=True)
            last_t1 = t1
            ot2tok[bk] = p.dv(lambda e, slw=slw, tc=tc: e.tensor_add(
                OT2[slw][:, tc * 512:(tc + 1) * 512],
                OT2[slw][:, tc * 512:(tc + 1) * 512],
                XT2[slw][:, tc * 512:(tc + 1) * 512]),
                waits=[xld[slw]], inc=True)
        wv_cons[slw] = mmv
        wq_cons[slw] = mmr
        st2_cons[slw] = [p.dma("sync", s_out_hbm[ot * 128:(ot + 1) * 128, :],
                               OT2[slw], waits=[ot2tok[0], ot2tok[1]])]
    p.barrier()


def ffn1_phase(C, base_out_hbm):
    import os
    FDBG = os.environ.get("FFN_DBG", "")
    p, PS = C.p, C.PS
    nc = C.nc
    Ff = C.F.rearrange("p a b -> p (a b)")
    FB = Ff.bitcast(BF16)
    sm = C.smalls
    zgd = nc.dram_tensor("zgd", [128, T], BF16, kind="Internal").ap()
    zud = nc.dram_tensor("zud", [128, T], BF16, kind="Internal").ap()
    zdsd = nc.dram_tensor("zdsd", [128, T], BF16, kind="Internal").ap()
    we8d = nc.dram_tensor("we8d", [8, T], F32, kind="Internal").ap()
    ZGW = FB[:, 8192:14336].rearrange("p (a b) -> p a b", b=T)
    ZUW = FB[:, 14336:20480].rearrange("p (a b) -> p a b", b=T)
    SL6 = Ff[:, 12288:14336]
    TG = SL6[:, 0:512]
    SG = SL6[:, 512:1024]
    TU = SL6[:, 1024:1536]
    HE = SL6[:, 0:512].bitcast(BF16)[:, 0:512]
    GATB = SL6[:, 1536:1792].bitcast(BF16)[:, 0:512]
    UPB = SL6[:, 1792:2048].bitcast(BF16)[:, 0:512]
    ZST = SL6[:, 1024:1536].bitcast(BF16)[:, 0:1024]
    H0 = C.W.rearrange("p a b -> p (a b)")[:, 0:22528].rearrange(
        "p (a b) -> p a b", b=512)
    WDQ2 = [C.W.rearrange("p a b -> p (a b)")[:, 22528:28160].rearrange(
        "p (a b) -> p a b", b=128),
        FB[:, 0:5632].rearrange("p (a b) -> p a b", b=128)]

    ZGB = SL6.bitcast(BF16)[:, 0:2048]
    WGR = [FB[:, i * 2048:(i + 1) * 2048].rearrange("p (a b) -> p a b", b=128)
           for i in range(2)]
    WUR = [FB[:, 4096 + i * 2048:4096 + (i + 1) * 2048].rearrange(
        "p (a b) -> p a b", b=128) for i in range(2)]
    tag = p.dma("gpsimd", WGR[0],
                C.tn["agT"].rearrange("(c p) m -> p c m", p=128))
    tau = p.dma("gpsimd", WUR[0],
                C.tn["auT"].rearrange("(c p) m -> p c m", p=128))
    for name, lhb, lt, dramt, dst16 in (("zg", WGR[0], tag, zgd, None),
                                        ("zu", WUR[0], tau, zud, None)):
        evt = []
        for tc in range(4):
            for db in range(NB):
                tok = p.pe(lambda e, db=db, tc=tc, lhb=lhb: e.matmul(
                    C.bank(tc)[:, 0:512], lhb[:, db, :],
                    C.R[:, db, tc * 512:(tc + 1) * 512],
                    start=(db == 0), stop=(db == NB - 1)),
                    waits=(C.smalls_tok + [lt]) if (db == 0) else [],
                    inc=(db == NB - 1))
            evt.append(p.dv(lambda e, tc=tc: e.tensor_scalar_mul(
                ZGB[:, tc * 512:(tc + 1) * 512], C.bank(tc)[:, 0:512], 2.0),
                waits=[tok], inc=True))
        t = p.dma("sync", dramt, ZGB, waits=evt)
        dstw = ZGW if name == "zg" else ZUW
        for e_ in range(NE):
            rr = (e_ % 3) * 32
            eb = e_ // 3
            p.dma("sync", dstw[rr:rr + 16, eb, :],
                  dramt[e_ * 16:(e_ + 1) * 16, :], waits=[t])
        p.barrier()
    WST = SL6[:, 0:128].rearrange("p (a b) -> p a b", b=8)
    gcol = SL6[:, 128:144]
    l2s = SL6[:, 144:152]
    eqs = SL6[:, 152:160]
    sr8 = sm["sr8f"].rearrange("p (c m) -> p c m", m=8)
    xn1fh = C.tn["xn1f"]
    xfr2 = [Ff[:, 0:2048], Ff[:, 2048:4096]]
    logacc2 = SL6[:, 160:288]
    rcons = [[], []]
    gacc = None
    for db in range(NB):
        sl = db % 2
        ld = p.dma("sync", xfr2[sl], xn1fh[db * 128:(db + 1) * 128, :],
                   waits=rcons[sl])
        for tt in range(NB):
            tok = p.pe(lambda e, tt=tt, db=db, sl=sl: e.matmul(
                C.bank(0)[:, tt * 8:(tt + 1) * 8],
                xfr2[sl][:, tt * 128:(tt + 1) * 128], sr8[:, db, :],
                start=True, stop=True),
                waits=[ld, gacc], inc=(tt == NB - 1))
        if db == 0:
            gacc = p.dv(lambda e: e.tensor_copy(logacc2, C.bank(0)[:, 0:128]),
                        waits=[tok], inc=True)
        else:
            gacc = p.dv(lambda e: e.tensor_add(logacc2, logacc2,
                                               C.bank(0)[:, 0:128]),
                        waits=[tok, gacc], strict=True, inc=True)
        rcons[sl] = [tok]
    lg = [gacc] * NB
    m1, m2, nm1, den = (gcol[:, 0:1], gcol[:, 1:2], gcol[:, 2:3], gcol[:, 3:4])
    wtr = []
    for tt in range(NB):
        L = logacc2[:, tt * 8:(tt + 1) * 8]
        t0 = p.dv(lambda e, L=L: e.reduce_max(m1, L, axis=AX.X),
                  waits=[lg[NB - 1]], inc=True)
        t1 = p.dv(lambda e, L=L: e.tensor_single_scalar(eqs, L, m1, ALU.is_ge),
                  waits=[t0], strict=True, inc=True)
        t2 = p.dv(lambda e, L=L: e.scalar_tensor_tensor(l2s, eqs, -1e30, L,
                                                        ALU.mult, ALU.add),
                  waits=[t1], strict=True, inc=True)
        t3 = p.dv(lambda e: e.reduce_max(m2, l2s, axis=AX.X),
                  waits=[t2], strict=True, inc=True)
        t4 = p.dv(lambda e: e.tensor_scalar_mul(nm1, m1, -1.0),
                  waits=[t3], strict=True, inc=True)
        ta = p.ac(lambda e, L=L: e.activation(l2s, L, AF.Exp, bias=nm1),
                  waits=[t4], inc=True)
        t5 = p.dv(lambda e: e.reduce_sum(den, l2s, axis=AX.X),
                  waits=[ta], strict=True, inc=True)
        t6 = p.dv(lambda e: e.reciprocal(den, den), waits=[t5], strict=True,
                  inc=True)
        t7 = p.dv(lambda e, L=L: e.tensor_single_scalar(eqs, L, m2, ALU.is_ge),
                  waits=[t6], strict=True, inc=True)
        t8 = p.dv(lambda e: e.tensor_mul(l2s, l2s, eqs), waits=[t7],
                  strict=True, inc=True)
        t9 = p.dv(lambda e, tt=tt: e.tensor_scalar(
            WST[:, tt, :], l2s, den, 2.0, ALU.mult, ALU.mult),
            waits=[t8], strict=True, inc=True)
        wtr.append(t9)
    WE8STG = Ff[64:72, 10240:12288]
    prev = []
    for tt in range(NB):
        t = p.pe(lambda e, tt=tt: e.transpose(
            PS[0:8, 3584:3712], WST[:, tt, :], sm["idn"]),
            waits=[wtr[tt]] + prev, inc=True)
        prev = [p.dv(lambda e, tt=tt: e.tensor_copy(
            WE8STG[:, tt * 128:(tt + 1) * 128], PS[0:8, 3584:3712]),
            waits=[t], inc=True)]
    p.dma("sync", we8d, WE8STG, waits=prev)
    p.barrier()
    if FDBG == "router":
        dbgf = nc.dram_tensor("dbgf", [8, T], F32, kind="ExternalOutput").ap()
        p.dma("sync", dbgf, WE8STG)
        p.barrier()
        return

    bgq = sm["bgq"].rearrange("p (b f) -> p b f", b=3)
    buq = sm["buq"].rearrange("p (b f) -> p b f", b=3)
    bg_h = C.tn["bgts"].rearrange("p (b f) -> p b f", b=3)
    bu_h = C.tn["buts"].rearrange("p (b f) -> p b f", b=3)
    bgq_cons = []
    bgq_tok = [None]

    def load_bgq(q):
        p.dma("gpsimd", bgq, bg_h[:, :, q * 1408:(q + 1) * 1408],
              waits=bgq_cons)
        p.dma("gpsimd", buq, bu_h[:, :, q * 1408:(q + 1) * 1408],
              waits=bgq_cons)
        bgq_tok[0] = p.dma(
            "gpsimd", sm["adq"].rearrange("p (a er) -> p a er", er=NE * R),
            adT_h[q * 1408:(q + 1) * 1408].rearrange(
                "(fb p) e r -> p fb (e r)", p=128), waits=bgq_cons)
    adtb = sm["adq"].rearrange("p (a e r) -> p a e r", e=NE, r=R)
    adT_h = C.tn["adT"]
    wg_cons = [[], []]
    wu_cons = [[], []]
    wg_tok = [None, None]
    wu_tok = [None, None]
    wgT, wuT, wdT = C.tn["wgT"], C.tn["wuT"], C.tn["wdT"]

    def load_wg(fb):
        slw = fb % 2
        wg_tok[slw] = p.dma("gpsimd", WGR[slw],
                            wgT[:, fb * 128:(fb + 1) * 128].rearrange(
                                "(c p) m -> p c m", p=128), waits=wg_cons[slw])
        wu_tok[slw] = p.dma("gpsimd", WUR[slw],
                            wuT[:, fb * 128:(fb + 1) * 128].rearrange(
                                "(c p) m -> p c m", p=128), waits=wu_cons[slw])

    HE8 = C.W.rearrange("p a b -> p (a b)")[:, 22528:26624].rearrange(
        "p (a b) -> p a b", b=512)
    TGb = [SL6[:, 0:256].bitcast(BF16), SL6[:, 256:512].bitcast(BF16)]
    SGb = [SL6[:, 512:768].bitcast(BF16), SL6[:, 768:1024].bitcast(BF16)]
    TUb = [SL6[:, 1024:1280].bitcast(BF16), SL6[:, 1280:1536].bitcast(BF16)]
    bank_user = {}
    buf_user = {}
    b7_user = [None]
    zd_ev = []
    wd_cons = []
    wdt = [None]
    st_base = []
    for tci in range(4):
        load_wg(0)
        h0toks = []
        glt = ult = None
        pending_zd = []
        for fb in range(NFB):
            slw = fb % 2
            if fb % 11 == 0:
                bgq_cons = [glt, ult] + [buf_user.get("he8_" + str(z))
                                         for z in range(NE)]
                load_bgq(fb // 11)
            if fb + 1 < NFB:
                load_wg(fb + 1)
            for db in range(NB):
                w = ([wg_tok[slw], bank_user.get(0)] if db == 0 else [])
                tokg = p.pe(lambda e, slw=slw, db=db, tci=tci: e.matmul(
                    C.bank(0)[:, 0:512], WGR[slw][:, db, :],
                    C.R[:, db, tci * 512:(tci + 1) * 512],
                    start=(db == 0), stop=(db == NB - 1)),
                    waits=w, inc=(db == NB - 1))
            for db in range(NB):
                w = ([wu_tok[slw], bank_user.get(1)] if db == 0 else [])
                toku = p.pe(lambda e, slw=slw, db=db, tci=tci: e.matmul(
                    C.bank(1)[:, 0:512], WUR[slw][:, db, :],
                    C.R[:, db, tci * 512:(tci + 1) * 512],
                    start=(db == 0), stop=(db == NB - 1)),
                    waits=w, inc=(db == NB - 1))
            wg_cons[slw] = [tokg]
            wu_cons[slw] = [toku]
            tgb = p.ac(lambda e: e.copy(GATB, C.bank(0)[:, 0:512]),
                       waits=[tokg, buf_user.get("gatb")], inc=True)
            tub = p.ac(lambda e: e.copy(UPB, C.bank(1)[:, 0:512]),
                       waits=[toku, buf_user.get("upb")], inc=True)
            tsg = p.ac(lambda e, fb=fb: e.activation(
                H0[:, fb, :], C.bank(0)[:, 0:512], AF.Silu),
                waits=[tokg] + wd_cons, inc=True)
            th0 = p.dv(lambda e, fb=fb: e.tensor_mul(
                H0[:, fb, :], H0[:, fb, :], C.bank(1)[:, 0:512]),
                waits=[tsg, toku], strict=True, inc=True)
            h0toks.append(th0)
            bank_user[0] = tsg
            bank_user[1] = th0
            for (ze, zfb, zthe) in pending_zd:
                rr2 = (ze % 3) * 32
                bkz2 = 4 + ze // 3
                buf_user["he8_" + str(ze)] = p.pe(
                    lambda e, ze=ze, zfb=zfb, bkz2=bkz2, rr2=rr2: e.matmul(
                        C.bank(bkz2)[rr2:rr2 + 16, 0:512],
                        adtb[:, zfb % 11, ze, :], HE8[:, ze, :],
                        start=(zfb == 0), stop=(zfb == NFB - 1),
                        tile_position=(0, rr2)),
                    waits=[zthe] + (zd_ev if zfb == 0 else []), inc=True)
            pending_zd = []
            for e_ in range(NE):
                par = e_ % 2
                rr = (e_ % 3) * 32
                fq = (fb % 11) * 128
                eb = e_ // 3
                bg_ = 0 if par == 0 else 2
                bu_ = 1 if par == 0 else 3
                w = [bank_user.get(bg_), bgq_tok[0]]
                glt = p.pe(lambda e, e_=e_, fq=fq, eb=eb, tci=tci, rr=rr,
                           bg_=bg_: e.matmul(
                    C.bank(bg_)[:, 0:512], bgq[rr:rr + 16, eb, fq:fq + 128],
                    ZGW[rr:rr + 16, eb, tci * 512:(tci + 1) * 512],
                    start=True, stop=True), waits=w, inc=True)
                w = [bank_user.get(bu_)]
                ult = p.pe(lambda e, e_=e_, fq=fq, eb=eb, tci=tci, rr=rr,
                           bu_=bu_: e.matmul(
                    C.bank(bu_)[:, 0:512], buq[rr:rr + 16, eb, fq:fq + 128],
                    ZUW[rr:rr + 16, eb, tci * 512:(tci + 1) * 512],
                    start=True, stop=True), waits=w, inc=True)
                ttg = p.dv(lambda e, par=par, bg_=bg_: e.tensor_tensor(
                    TGb[par], C.bank(bg_)[:, 0:512], GATB, ALU.add),
                    waits=[glt, tgb, buf_user.get("he" + str(par))], inc=True)
                ttu = p.dv(lambda e, par=par, bu_=bu_: e.tensor_tensor(
                    TUb[par], C.bank(bu_)[:, 0:512], UPB, ALU.add),
                    waits=[ult, tub, buf_user.get("tu" + str(par))], inc=True)
                bank_user[bg_] = ttg
                bank_user[bu_] = ttu
                tsg2 = p.ac(lambda e, par=par: e.activation(
                    SGb[par], TGb[par], AF.Silu),
                    waits=[ttg, buf_user.get("sgb" + str(par))], inc=True)
                the = p.dv(lambda e, par=par, e_=e_: e.tensor_mul(
                    HE8[:, e_, :], SGb[par], TUb[par]),
                    waits=[tsg2, ttu, buf_user.get("he8_" + str(e_))],
                    strict=True, inc=True)
                buf_user["tu" + str(par)] = the
                buf_user["sgb" + str(par)] = the
                buf_user["he" + str(par)] = the
                pending_zd.append((e_, fb, the))
            buf_user["gatb"] = ttg
            buf_user["upb"] = ttu
        for (ze, zfb, zthe) in pending_zd:
            rr2 = (ze % 3) * 32
            bkz2 = 4 + ze // 3
            buf_user["he8_" + str(ze)] = p.pe(
                lambda e, ze=ze, zfb=zfb, bkz2=bkz2, rr2=rr2: e.matmul(
                    C.bank(bkz2)[rr2:rr2 + 16, 0:512],
                    adtb[:, zfb % 11, ze, :], HE8[:, ze, :],
                    start=(zfb == 0), stop=(zfb == NFB - 1),
                    tile_position=(0, rr2)),
                waits=[zthe] + (zd_ev if zfb == 0 else []), inc=True)
        pending_zd = []
        stg = [SL6[:, 0:512], SL6[:, 512:1024]]
        wd_cons2 = [wd_cons, wg_cons[0] + wg_cons[1] + wu_cons[0] + wu_cons[1]]
        wdt2 = [None, None]

        def load_wd(dt_):
            sb2 = dt_ % 2
            wdt2[sb2] = p.dma("gpsimd", WDQ2[sb2],
                              wdT[:, dt_ * 128:(dt_ + 1) * 128].rearrange(
                                  "(c p) m -> p c m", p=128),
                              waits=wd_cons2[sb2] +
                              ([buf_user.get("he8_" + str(z))
                                for z in range(NE)] if dt_ <= 1 else []))

        load_wd(0)
        for dt_ in range(NB):
            sb = dt_ % 2
            if dt_ + 1 < NB:
                load_wd(dt_ + 1)
            for fs in range(NFB):
                w = [wdt2[sb], b7_user[0]] if fs == 0 else []
                tok = p.pe(lambda e, fs=fs, sb=sb: e.matmul(
                    C.bank(7)[:, 0:512], WDQ2[sb][:, fs, :], H0[:, fs, :],
                    start=(fs == 0), stop=(fs == NFB - 1)),
                    waits=w + ([h0toks[-1]] if fs == 0 else []),
                    inc=(fs == NFB - 1))
            wd_cons2[sb] = [tok]
            wd_cons = [tok]
            b7_user[0] = p.dv(lambda e, sb=sb: e.tensor_copy(
                stg[sb], C.bank(7)[:, 0:512]),
                waits=[tok] + st_base[-2:-1], inc=True)
            st_base.append(p.dma(
                "sync", base_out_hbm[dt_ * 128:(dt_ + 1) * 128,
                                     tci * 512:(tci + 1) * 512],
                stg[sb], waits=[b7_user[0]]))
        wg_cons[0] = wg_cons[0] + wd_cons2[1]
        wu_cons[0] = wu_cons[0] + wd_cons2[1]
        zd_ev = []
        prev_tz2 = None
        for e_ in range(NE):
            rr = (e_ % 3) * 32
            bkz = 4 + e_ // 3
            tz = p.dv(lambda e, e_=e_, rr=rr, bkz=bkz: e.tensor_copy(
                ZST[0:16, 0:512], C.bank(bkz)[rr:rr + 16, 0:512]),
                waits=[prev_tz2], inc=True)
            prev_tz2 = p.dma("sync",
                             zdsd[e_ * 16:(e_ + 1) * 16,
                                  tci * 512:(tci + 1) * 512],
                             ZST[0:16, 0:512], waits=[tz])
            zd_ev.append(prev_tz2)
        p.barrier()
    if FDBG == "base":
        return
    WE8 = Ff[0:8, 4096:6144]
    ZDST = SL6[:, 0:1024].bitcast(BF16)[:, 0:2048]
    ACC4 = [Ff[:, 6144:8192], Ff[:, 8192:10240]]
    t = p.dma("sync", WE8, we8d)
    t2 = p.dma("sync", ZDST, zdsd)
    p.barrier()
    sc_toks = []
    for tc in range(4):
        tb = p.pe(lambda e, tc=tc: e.matmul(
            C.bank(0)[:, 0:512], sm["sel16r"], WE8[:, tc * 512:(tc + 1) * 512],
            start=True, stop=True), waits=sc_toks, inc=True)
        sc_toks.append(p.dv(lambda e, tc=tc: e.tensor_mul(
            ZDST[:, tc * 512:(tc + 1) * 512], ZDST[:, tc * 512:(tc + 1) * 512],
            C.bank(0)[:, 0:512]), waits=[tb], strict=True, inc=True))
    bdtb = sm["bgq"][:, 0:2048]
    tbd = p.dma("gpsimd", bdtb, C.tn["bdT"])
    acc_user = [None, None]
    st_acc = []
    for dt_ in range(NB):
        sb = dt_ % 2
        for tc in range(4):
            tok = p.pe(lambda e, dt_=dt_, tc=tc: e.matmul(
                C.bank(1 + tc % 2)[:, 0:512],
                bdtb[:, dt_ * 128:(dt_ + 1) * 128],
                ZDST[:, tc * 512:(tc + 1) * 512],
                start=True, stop=True),
                waits=sc_toks + [acc_user[tc % 2], tbd], inc=True)
            acc_user[tc % 2] = p.dv(lambda e, sb=sb, tc=tc: e.tensor_copy(
                ACC4[sb][:, tc * 512:(tc + 1) * 512],
                C.bank(1 + tc % 2)[:, 0:512]),
                waits=[tok] + st_acc[-2:-1], inc=True)
        st_acc.append(p.dma("gpsimd",
                            base_out_hbm[dt_ * 128:(dt_ + 1) * 128, :],
                            ACC4[sb], waits=[acc_user[0], acc_user[1],
                                             st_base[-1]],
                            accum_op=ALU.add))
    p.barrier()


PHASE_IO = {
    "ln0": (["xT"], ["xn0b", "xn0f"]),
    "attn0": (["xn0b"], ["s0sc"]),
    "moe0": (["xn0b", "xn0f"], ["g0sc"]),
    "epi0": (["xT", "s0sc", "g0sc"], ["x1sc"]),
    "ln1": (["x1sc"], ["xn1b", "xn1f"]),
    "rwkv1": (["xn1f"], ["s1sc", "xinsc"]),
    "ffn1": (["xn1b", "xn1f"], ["base1"]),
    "epi1": (["x1sc", "s1sc", "base1"], ["out"]),
}

SMALLS_SPEC = {
    "bout": ([128, 16], F32, "bout", False, "g"),
    "bqkv": ([128, 48], F32, "bqkv", False, "g"),
    "bqkvb": ([128, 48], BF16, "bqkv", True, "g"),
    "beff": ([128, 1], F32, None, False, "g"),
    "fc1b": ([128, 128], F32, "fc1b", False, "ph"),
    "fc2b": ([8, D], BF16, "fc2b", True, "ph"),
    "alpha": ([1, 8], F32, "alpha", False, "g"),
    "alphat": ([128, 8], F32, None, False, "g"),
    "comb8b": ([8, T], BF16, None, False, "ph"),
    "sel": ([8, 1024], BF16, "sel", True, "ph"),
    "idn": ([128, 128], F32, "identity", False, "g"),
    "ln1w": ([128, 16], F32, "ln1w", False, "g"),
    "ln1b": ([128, 16], F32, "ln1b", False, "g"),
    "ln2w": ([128, 16], F32, "ln2w", False, "g"),
    "ln2b": ([128, 16], F32, "ln2b", False, "g"),
    "tmu": ([128, 16], F32, "tmu", False, "g"),
    "tmw": ([128, 16], F32, "tmw", False, "g"),
    "negw": ([128, 16], F32, None, False, "g"),
    "tmmixk": ([128, 16], F32, "tmmixk", False, "g"),
    "tmmixv": ([128, 16], F32, "tmmixv", False, "g"),
    "tmmixr": ([128, 16], F32, "tmmixr", False, "g"),
    "tmbo": ([128, 16], F32, "tmbo", False, "g"),
    "cmmixk": ([128, 16], F32, "cmmixk", False, "g"),
    "cmmixr": ([128, 16], F32, "cmmixr", False, "g"),
    "shiftz": ([128, 16], F32, None, False, "g"),
    "carrya": ([128, 16], F32, None, False, "g"),
    "carryb": ([128, 16], F32, None, False, "g"),
    "sr8f": ([128, 16 * 8], F32, "wr8T", False, "g"),
    "gatewf": ([128, 16 * 8], F32, "gatew", False, "g"),
    "bgq": ([128, 3 * 1408], BF16, None, False, "ph"),
    "buq": ([128, 3 * 1408], BF16, None, False, "ph"),
    "adq": ([128, 11 * NE * R], BF16, None, False, "ph"),
    "sel16r": ([8, 128], F32, "sel16r", False, "g"),
}
SM_PHASE = {"fc1b": 0, "fc2b": 0, "comb8b": 0, "sel": 0,
            "bgq": 1, "buq": 1, "adq": 1}
SM_ARENA_F32 = 4928
PHASE_SMALLS = {
    "ln0": [], "epi0": [], "epi1": [], "ln1": [],
    "attn0": ["bout", "bqkv", "bqkvb", "beff"],
    "moe0": ["fc1b", "fc2b", "alpha", "alphat", "comb8b", "sel",
             "idn", "gatewf"],
    "rwkv1": ["ln1w", "ln1b", "ln2w", "ln2b", "tmu", "tmw", "negw", "tmmixk",
              "tmmixv", "tmmixr", "tmbo", "cmmixk", "cmmixr", "shiftz",
              "carrya", "carryb"],
    "ffn1": ["sr8f", "bgq", "buq", "adq", "sel16r", "idn"],
}
PHASE_CCOLS = {
    "ln0": [1e-5], "ln1": [1e-5], "attn0": [SCALE_QK],
    "moe0": [], "epi0": [], "epi1": [], "rwkv1": [1e-5], "ffn1": [],
}
EXTRA_IN = {"attn0": ["nrmdram_decl"], "moe0": [], }


def build_launch(phases):
    nc = bass.Bass("TRN2", target_bir_lowering=False)
    p = Prog(nc)
    needs = []
    sin, sout = [], []
    produced = set()
    for ph in phases:
        for n in PHASE_NEEDS.get(ph, []):
            if n not in needs:
                needs.append(n)
        pin, pout = PHASE_IO[ph]
        for n in pin:
            if n != "xT" and n not in produced and n not in sin:
                sin.append(n)
        for n in pout:
            produced.add(n)
            if n not in sout:
                sout.append(n)
        if "xT" in pin and "xT" not in needs:
            needs.append("xT")
    consumed_later = {"xinsc"}
    for ph in phases:
        for n in PHASE_IO[ph][0]:
            consumed_later.add(n)
    sout_d = {}
    for n in sout:
        if n == "out":
            sout_d[n] = "ExternalOutput"
        elif n in consumed_later and len(phases) > 1:
            sout_d[n] = "Internal"
        else:
            sout_d[n] = "ExternalOutput"
    C = Dev(nc, p, needs, sin, sout_d)
    if any(ph == "attn0" for ph in phases):
        C.tn["nrmdram"] = nc.dram_tensor("nrmdram", [1, NH * T], F32,
                                         kind="Internal").ap()
    ccols = sorted({v for ph in phases for v in PHASE_CCOLS[ph]})
    C.ccol_init(ccols)
    C.smalls_tok = []
    sneed = []
    for ph in phases:
        for sm in PHASE_SMALLS[ph]:
            if sm not in sneed:
                sneed.append(sm)
    SM_REARR = {
        "sr8f": (lambda a: a.rearrange("(c p) m -> p c m", p=128),
                 lambda d: d.rearrange("p (c m) -> p c m", m=8)),
        "gatewf": (lambda a: a.rearrange("(c p) m -> p c m", p=128),
                   lambda d: d.rearrange("p (c m) -> p c m", m=8)),
        "agTb": (lambda a: a.rearrange("(c p) m -> p c m", p=128),
                 lambda d: d.rearrange("p (c m) -> p c m", m=128)),
        "auTb": (lambda a: a.rearrange("(c p) m -> p c m", p=128),
                 lambda d: d.rearrange("p (c m) -> p c m", m=128)),
        "adTb": (lambda a: a.rearrange("(fb p) e r -> p fb (e r)", p=128),
                 lambda d: d.rearrange("p (fb er) -> p fb er", er=NE * R)),
    }
    C.sm_arena = nc.alloc_sbuf_tensor("smArena", [128, SM_ARENA_F32], F32).ap()
    sm_off = {}
    for sm in sneed:
        shape, dt, src, cast, grp = SMALLS_SPEC[sm]
        if grp == "ph":
            ph_id = SM_PHASE[sm]
            off = sm_off.get(ph_id, 0)
            nfp32 = (shape[1] * (2 if dt == BF16 else 4) + 31) // 32 * 8
            flat = C.sm_arena.rearrange("p a -> p (a)") if False else C.sm_arena
            view = flat[:, off:off + nfp32]
            if dt == BF16:
                view = view.bitcast(BF16)
            C.smalls[sm] = view[0:shape[0], 0:shape[1]]
            sm_off[ph_id] = off + nfp32
            assert sm_off[ph_id] <= SM_ARENA_F32, (sm, sm_off)
        else:
            C.smalls[sm] = nc.alloc_sbuf_tensor("sm_" + sm, shape, dt).ap()
        if src is not None:
            sap = C.tn[src]
            dap = C.smalls[sm]
            if sm in SM_REARR:
                fs, fd = SM_REARR[sm]
                sap = fs(sap)
                dap = fd(dap)
            C.smalls_tok.append(p.dma("gpsimd" if cast else "sync",
                                      dap, sap))
    p.barrier()
    Rb = lambda db: C.R[:, db, :]
    for ph in phases:
        if ph == "ln0":
            ln_phase(C, C.tn["xT"], dst_sbuf=Rb, out_bf16=C.tn["xn0b"],
                     out_f32=C.tn["xn0f"], fp32_stats=True)
        elif ph == "attn0":
            if phases[0] == ph or "ln0" not in phases:
                loadR_phase(C, C.tn["xn0b"])
            attn0_phase(C, Rb, C.tn["s0sc"])
        elif ph == "moe0":
            loadR_phase(C, C.tn["xn0b"])
            moe0_phase(C, Rb, C.tn["g0sc"])
        elif ph == "epi0":
            epi_phase(C, C.tn["xT"], C.tn["s0sc"], C.tn["g0sc"], C.tn["x1sc"],
                      stats=("ln1" in phases))
        elif ph == "ln1":
            ln_phase(C, C.tn["x1sc"], dst_sbuf=Rb, out_bf16=C.tn["xn1b"],
                     out_f32=C.tn["xn1f"], fp32_stats=True,
                     skip_stats=("epi0" in phases))
        elif ph == "rwkv1":
            rwkv1_phase(C, C.tn["s1sc"])
        elif ph == "ffn1":
            loadR_phase(C, C.tn["xn1b"])
            ffn1_phase(C, C.tn["base1"])
        elif ph == "epi1":
            epi_phase(C, C.tn["x1sc"], C.tn["s1sc"], C.tn["base1"], C.tn["out"])
        else:
            raise NotImplementedError(ph)
        p.barrier()
    p.finalize()
    innames = needs + sin
    souts = [n for n in sout if sout_d[n] == "ExternalOutput"]
    return nc, innames, souts


FUSED = True
if FUSED:
    LAUNCH_PLAN = [["ln0", "attn0", "moe0", "epi0", "ln1", "rwkv1", "ffn1",
                    "epi1"]]
else:
    LAUNCH_PLAN = [["ln0"], ["attn0"], ["moe0"], ["epi0"], ["ln1"], ["rwkv1"],
                   ["ffn1"], ["epi1"]]


def kernel(**inputs):
    h = prep_host(inputs)
    h["sel"] = SELMAT
    per_core_x = h.pop("_x_per_core")
    state = {}
    for phases in LAUNCH_PLAN:
        nc, innames, outnames = build_launch(phases)
        in_maps = []
        for c in range(8):
            m = {}
            for n in innames:
                if n == "xT":
                    m[n] = per_core_x[c]
                elif n in h:
                    m[n] = np.ascontiguousarray(h[n], dtype=None)
                else:
                    m[n] = state[n][c]
            in_maps.append(m)
        res = run_bass_kernel_spmd(nc, in_maps, core_ids=list(range(8)))
        for n in outnames:
            state[n] = [np.asarray(res.results[c][n]) for c in range(8)]
    out = np.stack([state["out"][c].T for c in range(8)])
    return np.ascontiguousarray(out, dtype=np.float32)


SELMAT = np.zeros((8, 1024), dtype=np.float32)
for _e in range(8):
    SELMAT[_e, _e * 128:(_e + 1) * 128] = 1.0
IN_SPECS["sel"] = ([8, 1024], F32)
